# version 24
# baseline (speedup 1.0000x reference)
import contextlib
import numpy as np
import ml_dtypes
import concourse.bass as bass
import concourse.mybir as mybir
from concourse.bass_utils import run_bass_kernel_spmd

F32 = mybir.dt.float32
BF16 = mybir.dt.bfloat16
I32 = mybir.dt.int32
AF = mybir.ActivationFunctionType
ALU = mybir.AluOpType
AX = mybir.AxisListType

C = 2048
H = 8192
NH = 16
L = 64
TOK = 2048
HALF = 1024
NTB = 8
NF = C // 128
SCALE = float(1.0 / np.sqrt(128.0))
EPS_LN = 1e-5
EPS_RMS = 1e-8
HS = 1024
NSUP = H // HS
NHB = HS // 128
ENGINES = ("sync", "scalar", "vector", "tensor", "gpsimd")


class EM:
    def __init__(self, nc, sems):
        self.nc = nc
        self.sems = sems
        self.q = {e: [] for e in ENGINES}
        self.cnt = {k: 0 for k in sems}

    def on(self, eng, fn):
        self.q[eng].append(fn)

    def val(self, name):
        return self.cnt[name]

    def wait(self, eng, name, v):
        if v <= 0:
            return
        s = self.sems[name]
        self.on(eng, lambda e: e.wait_ge(s, v))

    def dma(self, eng, dst, src, name="LD", **kw):
        self.cnt[name] += 16
        v = self.cnt[name]
        s = self.sems[name]
        self.on(eng, lambda e: e.dma_start(dst, src, **kw).then_inc(s, 16))
        return v

    def step(self, eng, name, build):
        self.cnt[name] += 1
        v = self.cnt[name]
        s = self.sems[name]
        self.on(eng, lambda e: build(e).then_inc(s, 1))
        return v

    def run(self, eng, build):
        self.on(eng, lambda e: build(e))

    def fence_dma(self, engines=("sync", "vector", "scalar", "gpsimd")):
        for nm in ("PK", "OU", "GP"):
            v = self.cnt[nm]
            if v:
                for eng in engines:
                    self.wait(eng, nm, v)

    def flush(self):
        q = self.q
        self.q = {e: [] for e in ENGINES}
        if not any(q.values()):
            return
        with self.nc.Block(no_gpsimd_drain=True) as blk:
            for eng in ENGINES:
                fns = q[eng]
                if fns:
                    def make(fl):
                        def body(e):
                            for f in fl:
                                f(e)
                        return body
                    getattr(blk, eng)(make(fns))


def emit_trsN(em, groups, psA, psB, carry=None):
    ca, cb = carry if carry else (0, 0)
    vcop = {-2: ca, -1: cb}
    for i, (tfns, vfns) in enumerate(groups):
        bank = psA if i % 2 == 0 else psB
        prev = vcop.get(i - 2, 0)
        if prev:
            em.wait("tensor", "V", prev)
        vt = 0
        for j, tf in enumerate(tfns):
            if j == len(tfns) - 1:
                vt = em.step("tensor", "T", lambda e, f=tf, b=bank: f(e, b))
            else:
                em.run("tensor", lambda e, f=tf, b=bank: f(e, b))
        em.wait("vector", "T", vt)
        mk = 0
        for j, vf in enumerate(vfns):
            if j > 0:
                em.wait("vector", "V", mk)
            mk = em.step("vector", "V", lambda e, f=vf, b=bank: f(e, b))
        vcop[i] = mk
    n = len(groups)
    lastA = vcop[n - 2] if n % 2 == 0 else vcop[n - 1]
    lastB = vcop[n - 1] if n % 2 == 0 else vcop[n - 2]
    return (lastA, lastB), vcop[n - 1]


def emit_trs(em, items, psA, psB, carry=None):
    ca, cb = carry if carry else (0, 0)
    vcop = {-2: ca, -1: cb}
    for i, (tfn, vfn) in enumerate(items):
        bank = psA if i % 2 == 0 else psB
        prev = vcop.get(i - 2, 0)
        if prev:
            em.wait("tensor", "V", prev)
        vt = em.step("tensor", "T", lambda e, f=tfn, b=bank: f(e, b))
        em.wait("vector", "T", vt)
        if isinstance(vfn, (list, tuple)):
            mk = 0
            for j, vf in enumerate(vfn):
                if j > 0:
                    em.wait("vector", "V", mk)
                mk = em.step("vector", "V", lambda e, f=vf, b=bank: f(e, b))
            vcop[i] = mk
        else:
            vcop[i] = em.step("vector", "V", lambda e, f=vfn, b=bank: f(e, b))
    n = len(items)
    lastA = vcop[n - 2] if n % 2 == 0 else vcop[n - 1]
    lastB = vcop[n - 1] if n % 2 == 0 else vcop[n - 2]
    return (lastA, lastB), vcop[n - 1]


def emit_ln(em, st, src, dst, nrows, ncols, eps_ap, s_src=0):
    r = slice(0, nrows)
    if s_src:
        em.wait("scalar", "S", s_src)
    vs = em.step("scalar", "S", lambda e: e.activation(
        st["trash"][r, :ncols], src, AF.Square, accum_out=st["sq"][r, :1]))
    v1 = em.step("vector", "V", lambda e: e.reduce_sum(st["sm"][r, :1], src, axis=AX.X))
    inv_n = 1.0 / ncols
    em.wait("scalar", "V", v1)
    s1 = em.step("scalar", "S", lambda e: e.mul(st["mm"][r, :1], st["sm"][r, :1], inv_n))
    em.wait("scalar", "S", vs)
    s2 = em.step("scalar", "S", lambda e: e.mul(st["vv"][r, :1], st["sq"][r, :1], inv_n))
    em.wait("scalar", "S", s1)
    s3 = em.step("scalar", "S", lambda e: e.activation(
        st["r0"][r, :1], st["mm"][r, :1], AF.Square))
    em.wait("scalar", "S", s3)
    s5 = em.step("scalar", "S", lambda e: e.activation(
        st["sq"][r, :1], st["r0"][r, :1], AF.Identity, bias=st["vv"][r, :1],
        scale=-1.0))
    em.wait("scalar", "S", s5)
    s6 = em.step("scalar", "S", lambda e: e.activation(
        st["r0"][r, :1], st["sq"][r, :1], AF.Sqrt, bias=eps_ap[r, :1]))
    em.wait("vector", "S", s6)
    v2 = em.step("vector", "V", lambda e: e.reciprocal(st["rstd"][r, :1], st["r0"][r, :1]))
    em.wait("scalar", "V", v2)
    s7 = em.step("scalar", "S", lambda e: e.mul(st["mm"][r, :1], st["mm"][r, :1], -1.0))
    em.wait("scalar", "S", s7)
    s8 = em.step("scalar", "S", lambda e: e.mul(st["nbias"][r, :1], st["rstd"][r, :1],
                                                st["mm"][r, :1]))
    em.wait("scalar", "S", s8)
    return em.step("scalar", "S", lambda e: e.activation(
        dst, src, AF.Identity, bias=st["nbias"][r, :1], scale=st["rstd"][r, :1]))


def build_program(dbg=False):
    nc = bass.Bass("TRN2", target_bir_lowering=False)
    DK = "ExternalOutput" if dbg else "Internal"

    xs = nc.dram_tensor("xs", [TOK, C], F32, kind="ExternalInput")
    xhd = nc.dram_tensor("xh", [L, C], F32, kind="ExternalInput")
    latqT = nc.dram_tensor("latqT", [128, NH * L], F32, kind="ExternalInput")
    wkd = nc.dram_tensor("wk", [C, C], F32, kind="ExternalInput")
    wvd = nc.dram_tensor("wv", [C, C], F32, kind="ExternalInput")
    wpd = nc.dram_tensor("wp", [C, C], F32, kind="ExternalInput")
    wgad = nc.dram_tensor("wga", [C, NH * L], F32, kind="ExternalInput")
    wgTd = nc.dram_tensor("wgT", [H // 128, 128, C], F32, kind="ExternalInput")
    wuTd = nc.dram_tensor("wuT", [H // 128, 128, C], F32, kind="ExternalInput")
    wdTd = nc.dram_tensor("wdT", [H, C], F32, kind="ExternalInput")
    identf = nc.dram_tensor("identf", [128, 128], F32, kind="ExternalInput")
    identb = nc.dram_tensor("identb", [128, 128], BF16, kind="ExternalInput")
    onesf = nc.dram_tensor("onesf", [1, 128], F32, kind="ExternalInput")
    maskadd = nc.dram_tensor("maskadd", [L, L], F32, kind="ExternalInput")
    binmask = nc.dram_tensor("binmask", [L, L], F32, kind="ExternalInput")
    ropeC = nc.dram_tensor("ropeC", [L, C], F32, kind="ExternalInput")
    ropeS = nc.dram_tensor("ropeS", [L, C], F32, kind="ExternalInput")

    ys = nc.dram_tensor("ys", [TOK, C], F32, kind="ExternalOutput")

    apbuf_hi = nc.dram_tensor("apbuf_hi", [NH * L, C], BF16, kind=DK)
    apbuf_lo = nc.dram_tensor("apbuf_lo", [NH * L, C], BF16, kind=DK)
    x2buf = nc.dram_tensor("x2buf", [TOK, C], F32, kind=DK)
    wgac = nc.dram_tensor("wgac", [2, C, 1024], BF16, kind="Internal")
    if dbg:
        d_y1h = nc.dram_tensor("d_y1h", [L, C], F32, kind="ExternalOutput")
        d_k = nc.dram_tensor("d_k", [L, C], F32, kind="ExternalOutput")
        d_attnT = nc.dram_tensor("d_attnT", [128, NH * L], F32, kind="ExternalOutput")
        d_gth = nc.dram_tensor("d_gth", [128, 8 * HALF], BF16, kind="ExternalOutput")
        d_nT = nc.dram_tensor("d_nT", [128, NF * HALF], BF16, kind="ExternalOutput")
        d_sbc = nc.dram_tensor("d_sbc", [128, HALF], F32, kind="ExternalOutput")
        d_pth = nc.dram_tensor("d_pth", [128, NHB * HALF], BF16, kind="ExternalOutput")
        d_ptl = nc.dram_tensor("d_ptl", [128, NHB * HALF], BF16, kind="ExternalOutput")

    sems = {k: nc.alloc_semaphore(name="sem_" + k)
            for k in ("LD", "GP", "T", "V", "S", "XA0", "XA1", "WG0", "WG1", "WG2", "WG3",
                      "UP0", "UP1", "UP2", "WD0", "WD1", "WP0", "WP1",
                      "KV0", "KV1", "PK", "OU", "GC")}
    em = EM(nc, sems)

    gst = contextlib.ExitStack()
    gsb = lambda n, s, d=F32: gst.enter_context(nc.sbuf_tensor(n, s, d))

    identf_t = gsb("identf_t", [128, 128])
    identb_t = gsb("identb_t", [128, 128], BF16)
    onesf_t = gsb("onesf_t", [1, 128])
    st = {k: gsb("st_" + k, [128, 1]) for k in
          ("sq", "sm", "mm", "vv", "r0", "rstd", "nbias", "am", "invs")}
    st["trash"] = gsb("st_trash", [128, C])
    eps5_t = gsb("eps5_t", [128, 1])
    eps8_t = gsb("eps8_t", [128, 1])
    stB = {k: gsb("stB_" + k, [128, 1]) for k in
           ("sq", "sm", "mm", "vv", "r0", "rstd", "nbias", "am", "invs")}
    stB["trash"] = st["trash"]
    st2 = [st, stB]

    v0 = None
    for dst, srck in [(identf_t[:], identf[:, :]), (identb_t[:], identb[:, :]),
                      (onesf_t[:], onesf[:, :])]:
        v0 = em.dma("sync", dst, srck, "LD")
    em.wait("sync", "LD", v0)
    em.run("vector", lambda e: e.memset(eps5_t[:, :1], EPS_LN))
    em.run("vector", lambda e: e.memset(eps8_t[:, :1], EPS_RMS))
    em.flush()

    with contextlib.ExitStack() as ps:
        psb = lambda n, s, d=F32: ps.enter_context(nc.sbuf_tensor(n, s, d))
        pps = lambda n, s, d=F32: ps.enter_context(nc.psum_tensor(n, s, d))
        maskadd_t = psb("maskadd_t", [L, L])
        binmask_t = psb("binmask_t", [L, L])
        ropeC_t = psb("ropeC_t", [L, C])
        ropeS_t = psb("ropeS_t", [L, C])
        xh_t = psb("xh_t", [L, C])
        y1h = psb("y1h", [L, C])
        y1hT = psb("y1hT", [128, NF * L])
        kbuf = psb("kbuf", [L, C])
        vbuf = psb("vbuf", [L, C])
        ksw = psb("ksw", [L, C])
        ktmp = psb("ktmp", [L, C])
        kT = psb("kT", [128, NH * L])
        qT = psb("qT", [128, NH * L])
        smb = psb("smb", [L, L])
        wb = psb("wb", [L, L])
        wTs = psb("wTs", [L, NH * L])
        attnT = psb("attnT", [128, NH * L])
        wchunk = psb("wchunk", [128, 2 * C])
        aph = psb("aph", [128, 8 * C], BF16)
        apl = psb("apl", [128, 8 * C], BF16)
        pbig = pps("pbig", [L, C])
        trstk = contextlib.ExitStack()
        pstA = trstk.enter_context(nc.psum_tensor("pstA", [128, 512], F32))
        pstB = trstk.enter_context(nc.psum_tensor("pstB", [128, 512], F32))

        for dst, srck in [(maskadd_t[:], maskadd[:, :]), (binmask_t[:], binmask[:, :]),
                          (ropeC_t[:], ropeC[:, :]), (ropeS_t[:], ropeS[:, :])]:
            em.dma("sync", dst, srck, "LD")
        em.dma("sync", xh_t[:], xhd[:, :], "LD")
        vl = em.dma("sync", qT[:, :], latqT[:, :], "LD")
        em.wait("scalar", "LD", vl)
        em.wait("vector", "LD", vl)
        mln1h = emit_ln(em, st, xh_t[:L, :], y1h[:L, :], L, C, eps5_t)
        em.wait("tensor", "S", mln1h)

        carP, vy1t = emit_trs(em, [
            (lambda e, b, c=c: e.transpose(
                b[:, :64], y1h[:L, c * 128:(c + 1) * 128], identf_t[:L, :L]),
             lambda e, b, c=c: e.tensor_copy(y1hT[:, c * L:(c + 1) * L], b[:, :64]))
            for c in range(NF)], pstA, pstB)
        em.wait("tensor", "V", vy1t)

        kv_T = {}
        for wi, (wmat, dst) in enumerate(((wkd, kbuf), (wvd, vbuf))):
            for f in range(NF):
                gi = wi * NF + f
                slot = (gi % 2) * C
                if gi >= 2:
                    em.wait("sync", "T", kv_T[gi - 2])
                vw = em.dma("sync", wchunk[:, slot:slot + C],
                            wmat[f * 128:(f + 1) * 128, :], "KV%d" % (gi % 2))
                em.wait("tensor", "KV%d" % (gi % 2), vw)
                for ot in range(4):
                    mk = (lambda e, f=f, ot=ot, slot=slot: e.matmul(
                        pbig[:L, ot * 512:(ot + 1) * 512],
                        y1hT[:, f * L:(f + 1) * L],
                        wchunk[:, slot + ot * 512:slot + (ot + 1) * 512],
                        start=(f == 0), stop=(f == NF - 1)))
                    if ot == 3:
                        kv_T[gi] = em.step("tensor", "T", mk)
                    else:
                        em.run("tensor", mk)
            em.wait("vector", "T", kv_T[wi * NF + NF - 1])
            vkv = em.step("vector", "V", lambda e, dst=dst: e.tensor_copy(
                dst[:L, :], pbig[:L, :]))
            em.wait("tensor", "V", vkv)
            if wi == 0:
                vkv_k = vkv

        em.wait("vector", "V", vkv_k)
        em.run("vector", lambda e: e.tensor_copy(
            bass.AP(ksw, 0, [[C, L], [2, C // 2]]),
            bass.AP(kbuf, 1, [[C, L], [2, C // 2]])))
        vr1 = em.step("vector", "V", lambda e: e.tensor_copy(
            bass.AP(ksw, 1, [[C, L], [2, C // 2]]),
            bass.AP(kbuf, 0, [[C, L], [2, C // 2]])))
        em.run("vector", lambda e: e.tensor_mul(ktmp[:L, :], kbuf[:L, :], ropeC_t[:L, :]))
        em.wait("vector", "V", vr1)
        vr2 = em.step("vector", "V", lambda e: e.tensor_mul(
            ksw[:L, :], ksw[:L, :], ropeS_t[:L, :]))
        em.wait("vector", "V", vr2)
        vrk = em.step("vector", "V", lambda e: e.tensor_add(
            kbuf[:L, :], ktmp[:L, :], ksw[:L, :]))
        em.wait("tensor", "V", vrk)

        carP, vkt = emit_trs(em, [
            (lambda e, b, c=c: e.transpose(
                b[:, :64], kbuf[:L, c * 128:(c + 1) * 128], identf_t[:L, :L]),
             lambda e, b, c=c: e.tensor_copy(kT[:, c * L:(c + 1) * L], b[:, :64]))
            for c in range(NF)], pstA, pstB, carry=carP)
        em.flush()

        trstk.close()
        hstk = contextlib.ExitStack()
        pss = hstk.enter_context(nc.psum_tensor("pss", [L, 512], F32))
        psa = hstk.enter_context(nc.psum_tensor("psa", [128, 512], F32))
        pss2 = hstk.enter_context(nc.psum_tensor("pss2", [L, 512], F32))
        psa2 = hstk.enter_context(nc.psum_tensor("psa2", [128, 512], F32))

        m_sc = {}
        m_at = {}
        t_sc = {}
        vs_prev = 0
        em.wait("tensor", "V", vkt)
        em.wait("tensor", "LD", vl)
        for h in range(NH):
            hsl = slice(h * L, (h + 1) * L)
            pb = pss if h % 2 == 0 else pss2
            pa = psa if h % 2 == 0 else psa2
            if h >= 2:
                em.wait("tensor", "V", m_sc[h - 2])
            vt = em.step("tensor", "T", lambda e, hsl=hsl, pb=pb: e.matmul(
                pb[:L, :L], qT[:, hsl], kT[:, hsl], start=True, stop=True))
            em.wait("vector", "T", vt)
            em.wait("vector", "S", vs_prev)
            vsm = em.step("vector", "V", lambda e, pb=pb: e.scalar_tensor_tensor(
                smb[:L, :L], pb[:L, :L], SCALE, maskadd_t[:L, :L], ALU.mult, ALU.add))
            em.wait("vector", "V", vsm)
            vv = em.step("vector", "V", lambda e: e.reduce_max(
                st["mm"][:L, :1], smb[:L, :L], axis=AX.X, negate=True))
            em.wait("scalar", "V", vv)
            vs = em.step("scalar", "S", lambda e: e.activation(
                wb[:L, :L], smb[:L, :L], AF.Exp, bias=st["mm"][:L, :1]))
            vs_prev = vs
            em.wait("vector", "S", vs)
            vbm = em.step("vector", "V", lambda e: e.tensor_mul(
                wb[:L, :L], wb[:L, :L], binmask_t[:L, :L]))
            em.wait("vector", "V", vbm)
            vrs = em.step("vector", "V", lambda e: e.reduce_sum(
                st["sm"][:L, :1], wb[:L, :L], axis=AX.X))
            em.wait("vector", "V", vrs)
            vrc = em.step("vector", "V", lambda e: e.reciprocal(
                st["r0"][:L, :1], st["sm"][:L, :1]))
            em.wait("vector", "V", vrc)
            vv2 = em.step("vector", "V", lambda e: e.tensor_scalar(
                wb[:L, :L], wb[:L, :L], st["r0"][:L, :1], None, ALU.mult))
            em.wait("tensor", "V", vv2)
            vt2 = em.step("tensor", "T", lambda e, pb=pb: e.transpose(
                pb[:L, 64:64 + L], wb[:L, :L], identf_t[:L, :L]))
            t_sc[h] = vt2
            em.wait("vector", "T", vt2)
            m_sc[h] = em.step("vector", "V", lambda e, hsl=hsl, pb=pb: e.tensor_copy(
                wTs[:L, hsl], pb[:L, 64:64 + L]))
            em.wait("tensor", "V", m_sc[h])
            if h >= 2:
                em.wait("tensor", "V", m_at[h - 2])
            vt3 = em.step("tensor", "T", lambda e, h=h, hsl=hsl, pa=pa: e.matmul(
                pa[:, :L], vbuf[:L, h * 128:(h + 1) * 128], wTs[:L, hsl],
                start=True, stop=True))
            em.wait("vector", "T", vt3)
            m_at[h] = em.step("vector", "V", lambda e, hsl=hsl, pa=pa: e.tensor_copy(
                attnT[:, hsl], pa[:, :L]))
        em.wait("tensor", "V", m_at[NH - 1])

        hstk.close()
        pbig2 = pps("pbig2", [L, C])
        pbv = [pbig, pbig2]
        m_ap = {}
        s_ap = {}
        wp_ld = {}
        for h in range(NH):
            slot = (h % 2) * C
            if h >= 2:
                em.wait("sync", "T", tconsumed[h - 2]) if False else None
            wp_ld[h] = em.dma("sync", wchunk[:, slot:slot + C],
                              wpd[h * 128:(h + 1) * 128, :], "WP%d" % (h % 2))
            if h == 1:
                break
        for h in range(NH):
            slot = (h % 2) * C
            em.wait("tensor", "WP%d" % (h % 2), wp_ld[h])
            if h >= 2:
                em.wait("tensor", "V", m_ap[h - 2])
                em.wait("tensor", "S", s_ap[h - 2])
            prow = (h % 2) * 64
            pcol = (h // 2) * C
            vt = None
            pb_ = pbv[h % 2]
            for dm in range(4):
                mk = (lambda e, h=h, dm=dm, slot=slot, pb_=pb_: e.matmul(
                    pb_[:L, dm * 512:(dm + 1) * 512],
                    attnT[:, h * L:(h + 1) * L],
                    wchunk[:, slot + dm * 512:slot + (dm + 1) * 512],
                    start=True, stop=True))
                if dm == 3:
                    vt = em.step("tensor", "T", mk)
                else:
                    em.run("tensor", mk)
            if h + 2 < NH:
                em.wait("sync", "T", vt)
                wp_ld[h + 2] = em.dma("sync", wchunk[:, ((h + 2) % 2) * C:((h + 2) % 2) * C + C],
                                      wpd[(h + 2) * 128:(h + 3) * 128, :], "WP%d" % (h % 2))
            em.wait("scalar", "T", vt)
            s_ap[h] = em.step("scalar", "S", lambda e, prow=prow, pcol=pcol, pb_=pb_: e.copy(
                aph[prow:prow + L, pcol:pcol + C], pb_[:L, :]))
            em.wait("vector", "S", s_ap[h])
            em.wait("vector", "T", vt)
            m_ap[h] = em.step("vector", "V", lambda e, prow=prow, pcol=pcol, pb_=pb_: e.scalar_tensor_tensor(
                apl[prow:prow + L, pcol:pcol + C], pb_[:L, :], 1.0,
                aph[prow:prow + L, pcol:pcol + C], ALU.mult, ALU.subtract))
        for j in range(8):
            em.wait("gpsimd", "V", m_ap[2 * j + 1])
            em.wait("gpsimd", "S", s_ap[2 * j + 1])
            dj = [[C, 128], [1, C]]
            em.dma("gpsimd", bass.AP(apbuf_hi, j * 128 * C, dj),
                   bass.AP(aph, j * C, [[8 * C, 128], [1, C]]), "GP")
            em.dma("gpsimd", bass.AP(apbuf_lo, j * 128 * C, dj),
                   bass.AP(apl, j * C, [[8 * C, 128], [1, C]]), "GP")
        em.flush()

    out_dma_total = 0
    for half in range(2):
        t0 = half * HALF
        with contextlib.ExitStack() as hst:
            hsb = lambda n, s, d=F32: hst.enter_context(
                nc.sbuf_tensor("%s_h%d" % (n, half), s, d))
            nT = hsb("nT", [128, NF * HALF], BF16)
            sbc = hsb("sbc", [128, HALF])
            srow = hsb("srow", [1, HALF])
            gtstk = contextlib.ExitStack()
            gsb2 = lambda n, s, d=F32: gtstk.enter_context(
                nc.sbuf_tensor("%s_h%d" % (n, half), s, d))
            gth = gsb2("gth", [128, 8 * HALF], BF16)
            gtl = gsb2("gtl", [128, 8 * HALF], BF16)
            gth_ref = gth

            em.fence_dma()
            with contextlib.ExitStack() as s2:
                sb2 = lambda n, s, d=F32: s2.enter_context(
                    nc.sbuf_tensor("%s_h%da" % (n, half), s, d))
                pp2 = lambda n, s, d=F32: s2.enter_context(
                    nc.psum_tensor("%s_h%da" % (n, half), s, d))
                wgah = sb2("wgah", [128, NF * 1024], BF16)
                wgal = sb2("wgal", [128, NF * 1024], BF16)
                wstg = sb2("wstg", [128, 2 * 1024])
                xblk = sb2("xblk", [128, C])
                xblk2 = sb2("xblk2", [128, C])
                y1 = sb2("y1", [128, C])
                y1b = sb2("y1b", [128, C])
                y1Th = sb2("y1Th", [128, C], BF16)
                y1Tl = sb2("y1Tl", [128, C], BF16)
                ebuf = sb2("ebuf", [128, 1024])
                gbuf = sb2("gbuf", [128, 1024])
                red = sb2("red", [128, 16])
                pstA2 = pp2("pstA2", [128, 512])
                pstB2 = pp2("pstB2", [128, 512])
                pslg = pp2("pslg", [128, 1024])
                pslg2 = pp2("pslg2", [128, 1024])

                m_wsp = {}
                wg_sc = {}
                wga_ld = 0
                slab_d = [[16 * 1024, 128], [1024, 16], [1, 1024]]
                dram_d = [[1024, 128], [128 * 1024, 16], [1, 1024]]

                def emit_wgasplit():
                    slots = [(wstg, 0), (wstg, 1024), (ebuf, 0), (gbuf, 0)]
                    for f in range(NF):
                        buf, off = slots[f % 4]
                        if f >= 4:
                            em.wait("sync", "S", wg_sc[f - 4])
                            em.wait("sync", "V", m_wsp[f - 4])
                        vw = em.dma("sync", buf[:, off:off + 1024],
                                    wgad[f * 128:(f + 1) * 128, :], "WG%d" % (f % 4))
                        em.wait("scalar", "WG%d" % (f % 4), vw)
                        wg_sc[f] = em.step("scalar", "S", lambda e, f=f, buf=buf, off=off: e.copy(
                            wgah[:, f * 1024:(f + 1) * 1024], buf[:, off:off + 1024]))
                        em.wait("vector", "S", wg_sc[f])
                        m_wsp[f] = em.step("vector", "V", lambda e, f=f, buf=buf, off=off: e.scalar_tensor_tensor(
                            wgal[:, f * 1024:(f + 1) * 1024], buf[:, off:off + 1024], 1.0,
                            wgah[:, f * 1024:(f + 1) * 1024], ALU.mult, ALU.subtract))
                    em.wait("gpsimd", "V", m_wsp[NF - 1])
                    em.wait("gpsimd", "S", wg_sc[NF - 1])
                    em.dma("gpsimd", bass.AP(wgac, 0, dram_d),
                           bass.AP(wgah, 0, slab_d), "GP")
                    em.dma("gpsimd", bass.AP(wgac, C * 1024, dram_d),
                           bass.AP(wgal, 0, slab_d), "GP")
                if half == 1:
                    wga_ld0 = em.dma("sync", bass.AP(wgah, 0, slab_d),
                                     bass.AP(wgac, 0, dram_d), "WG0")
                    wga_ld = em.dma("sync", bass.AP(wgal, 0, slab_d),
                                    bass.AP(wgac, C * 1024, dram_d), "WG1")
                    for f in range(NF):
                        m_wsp[f] = 0

                xbv = [xblk, xblk2]
                y1v = [y1, y1b]
                y1Thv = [y1Th, y1Th]
                y1Tlv = [y1Tl, y1Tl]
                ebv = [ebuf, ebuf]
                gbv = [gbuf, gbuf]
                redv = [red, red]
                pslgv = [pslg, pslg2]
                m_aff = {}; m_sp = {}; m_lg = {}; m_rm = {}; m_ex = {}
                m_gb = {}; t_y1 = {}; t_gt = {}
                vx = {}
                carH = {'c': None}

                def dma_x2a(tb):
                    if tb >= NTB:
                        return
                    p = tb % 2
                    if tb >= 2:
                        em.wait("sync", "S", m_aff[tb - 2])
                    vx[tb] = em.dma("sync", xbv[p][:],
                                    xs[t0 + tb * 128:t0 + tb * 128 + 128, :], "XA%d" % p)
                dma_x2a(0)
                dma_x2a(1)

                def partLN(tb):
                    if tb >= NTB:
                        return
                    p = tb % 2
                    em.wait("scalar", "XA%d" % p, vx[tb])
                    em.wait("vector", "XA%d" % p, vx[tb])
                    if tb >= 2:
                        em.wait("scalar", "T", t_y1.get(tb - 2, 0))
                    m_aff[tb] = emit_ln(em, st2[p], xbv[p][:, :], y1v[p][:, :], 128, C, eps5_t)
                    dma_x2a(tb + 2)

                def partTrs2a(tb):
                    p = tb % 2
                    em.wait("tensor", "S", m_aff[tb])
                    if tb >= 1:
                        em.wait("vector", "T", m_lg.get(tb - 1, 0))
                    carH['c'], m_sp[tb] = emit_trsN(em, [
                        ([(lambda e, b, c=4 * g + j, j=j, p=p: e.transpose(
                              b[:, j * 128:(j + 1) * 128],
                              y1v[p][:, c * 128:(c + 1) * 128], identf_t[:, :]))
                          for j in range(4)],
                         [lambda e, b, c0=g * 512, p=p: e.tensor_copy(
                             y1Thv[p][:, c0:c0 + 512], b[:, :512]),
                          lambda e, b, c0=g * 512, p=p: e.scalar_tensor_tensor(
                             y1Tlv[p][:, c0:c0 + 512], b[:, :512], 1.0,
                             y1Thv[p][:, c0:c0 + 512], ALU.mult, ALU.subtract)])
                        for g in range(4)], pstA2, pstB2, carry=carH['c'])
                    t_y1[tb] = em.val("T")
                def partA2(tb):
                    p = tb % 2
                    em.wait("tensor", "V", m_sp[tb])
                    if tb >= 2:
                        em.wait("tensor", "S", m_ex.get(tb - 2, 0))
                        em.wait("tensor", "V", m_rm.get(tb - 2, 0))
                    for ot in range(2):
                        for f in range(NF):
                            if tb == 0 and ot == 0:
                                if half == 0:
                                    em.wait("tensor", "S", wg_sc[f])
                                    em.wait("tensor", "V", m_wsp[f])
                                elif f == 0:
                                    em.wait("tensor", "WG0", wga_ld0)
                                    em.wait("tensor", "WG1", wga_ld)
                            for pp_, (lh, rh) in enumerate(
                                    ((y1Thv[p], wgah), (y1Thv[p], wgal), (y1Tlv[p], wgah))):
                                mk = (lambda e, ot=ot, f=f, pp_=pp_, lh=lh, rh=rh, p=p: e.matmul(
                                    pslgv[p][:, ot * 512:(ot + 1) * 512],
                                    lh[:, f * 128:(f + 1) * 128],
                                    rh[:, f * 1024 + ot * 512:f * 1024 + (ot + 1) * 512],
                                    start=(f == 0 and pp_ == 0),
                                    stop=(f == NF - 1 and pp_ == 2)))
                                if ot == 1 and f == NF - 1 and pp_ == 2:
                                    m_lg[tb] = em.step("tensor", "T", mk)
                                else:
                                    em.run("tensor", mk)

                def partBsm(tb):
                    p = tb % 2
                    em.wait("vector", "T", m_lg[tb])
                    em.wait("vector", "S", m_ex.get(tb - 1, 0))
                    m_rm[tb] = em.step("vector", "V", lambda e, p=p: e.reduce_max(
                        bass.AP(redv[p], 0, [[16, 128], [1, 16], [0, 1]]),
                        bass.AP(pslgv[p], 0, [[1024, 128], [64, 16], [1, 64]]),
                        axis=AX.X, negate=True))
                    em.wait("vector", "V", m_rm[tb])
                    vng = em.step("vector", "V", lambda e, p=p: e.tensor_scalar(
                        redv[p][:, :16], redv[p][:, :16], SCALE, None, ALU.mult))
                    em.wait("scalar", "V", vng)
                    em.wait("scalar", "V", m_gb.get(tb - 1, 0))
                    for g in range(16):
                        mk = (lambda e, g=g, p=p: e.activation(
                            ebv[p][:, g * 64:(g + 1) * 64], pslgv[p][:, g * 64:(g + 1) * 64],
                            AF.Exp, bias=redv[p][:, g:g + 1], scale=SCALE))
                        if g == 15:
                            m_ex[tb] = em.step("scalar", "S", mk)
                        else:
                            em.run("scalar", mk)
                    em.wait("vector", "S", m_ex[tb])
                    vsu = em.step("vector", "V", lambda e, p=p: e.reduce_sum(
                        bass.AP(redv[p], 0, [[16, 128], [1, 16], [0, 1]]),
                        bass.AP(ebv[p], 0, [[1024, 128], [64, 16], [1, 64]]), axis=AX.X))
                    em.wait("vector", "V", vsu)
                    vrc = em.step("vector", "V", lambda e, p=p: e.reciprocal(
                        redv[p][:, :16], redv[p][:, :16]))
                    em.wait("vector", "V", vrc)
                    if tb >= 1:
                        em.wait("vector", "T", t_gt.get(tb - 1, 0))
                    for g in range(16):
                        mk = (lambda e, g=g, p=p: e.tensor_scalar(
                            gbv[p][:, g * 64:(g + 1) * 64], ebv[p][:, g * 64:(g + 1) * 64],
                            redv[p][:, g:g + 1], None, ALU.mult))
                        if g == 15:
                            m_gb[tb] = em.step("vector", "V", mk)
                        else:
                            em.run("vector", mk)
                def partBgt(tb):
                    p = tb % 2
                    em.wait("tensor", "V", m_gb[tb])
                    def gt_dst(t, g, tb=tb):
                        return bass.AP(t, (4 * g) * HALF + tb * 128,
                                       [[8 * HALF, 128], [HALF, 4], [1, 128]])
                    bsrc = lambda b: bass.AP(b, 0, [[512, 128], [128, 4], [1, 128]])
                    carH['c'], _ = emit_trsN(em, [
                        ([(lambda e, b, q=4 * g + j, j=j, p=p: e.transpose(
                              b[:, j * 128:(j + 1) * 128],
                              gbv[p][:, q * 128:(q + 1) * 128], identf_t[:, :]))
                          for j in range(4)],
                         [lambda e, b, g=g, f_=gt_dst, s_=bsrc: e.tensor_copy(
                              f_(gth, g), s_(b)),
                          lambda e, b, g=g, f_=gt_dst, s_=bsrc: e.scalar_tensor_tensor(
                              f_(gtl, g), s_(b), 1.0, f_(gth, g),
                              ALU.mult, ALU.subtract)])
                        for g in range(2)], pstA2, pstB2, carry=carH['c'])
                    t_gt[tb] = em.val("T")

                partLN(0)
                if half == 0:
                    emit_wgasplit()
                for k in range(NTB + 1):
                    partLN(k + 1)
                    if k < NTB:
                        partTrs2a(k)
                    if k >= 1:
                        partBsm(k - 1)
                    if k < NTB:
                        partA2(k)
                    if k >= 1:
                        partBgt(k - 1)
                em.flush()

            if dbg and half == 0:
                em.dma("gpsimd", d_gth[:, :], gth_ref[:, :], "GP")
                em.flush()

            with contextlib.ExitStack() as s2:
                sb2 = lambda n, s, d=F32: s2.enter_context(
                    nc.sbuf_tensor("%s_h%db" % (n, half), s, d))
                pp2 = lambda n, s, d=F32: s2.enter_context(
                    nc.psum_tensor("%s_h%db" % (n, half), s, d))
                aph = sb2("aph", [128, 8 * C], BF16)
                apl = sb2("apl", [128, 8 * C], BF16)
                xblk = sb2("xblk", [128, C])
                xblk2 = sb2("xblk2", [128, C])
                xob = sb2("xob", [128, C])
                x2b = sb2("x2b", [128, C])
                y2b = sb2("y2b", [128, C])
                y3b = sb2("y3b", [128, C])
                nib = sb2("nib", [128, 1024], I32)
                nbb = sb2("nbb", [128, C], BF16)
                nbb2 = sb2("nbb2", [128, C], BF16)
                psxo = pp2("psxo", [128, C])
                pstnA = pp2("pstnA", [128, 1024], BF16)
                pstnB = pp2("pstnB", [128, 1024], BF16)
                psr = pp2("psr", [128, 512])

                dsta = [[8 * C, 128], [C, 4], [1, C]]
                srca = [[C, 128], [128 * C, 4], [1, C]]
                em.wait("sync", "GP", em.val("GP"))
                em.dma("sync", bass.AP(aph, 0, dsta), bass.AP(apbuf_hi, 0, srca), "LD")
                vap1 = em.dma("sync", bass.AP(apl, 0, dsta), bass.AP(apbuf_lo, 0, srca), "LD")
                em.dma("sync", bass.AP(aph, 4 * C, dsta),
                       bass.AP(apbuf_hi, 4 * 128 * C, srca), "KV0")
                vap2 = em.dma("sync", bass.AP(apl, 4 * C, dsta),
                              bass.AP(apbuf_lo, 4 * 128 * C, srca), "KV0")

                xbv = [xblk, xblk2]
                pkh = {'v': 0}
                m_add = {}; m_ln2 = {}; m_ln3 = {}; m_rms = {}
                m_xo_mm = {}; m_xo_ev = {}; m_nbb = {}; m_psr = {}
                t_nT = {}
                vx = {}
                carN = {'c': None}

                def dma_x2b(tb):
                    if tb >= NTB:
                        return
                    p = tb % 2
                    if tb >= 2:
                        em.wait("sync", "V", m_add.get(tb - 2, 0))
                    vx[tb] = em.dma("sync", xbv[p][:],
                                    xs[t0 + tb * 128:t0 + tb * 128 + 128, :], "XA%d" % p)
                dma_x2b(0)
                dma_x2b(1)

                def partA2b(tb):
                    p = tb % 2
                    for dm in range(4):
                        if tb >= 1:
                            nm_, mk_ = m_xo_ev[(tb - 1, dm)]
                            em.wait("tensor", nm_, mk_)
                        for q in range(8):
                            col = q * HALF + tb * 128
                            if tb == 0 and dm == 0:
                                if q == 0:
                                    em.wait("tensor", "LD", vap1)
                                elif q == 4:
                                    em.wait("tensor", "KV0", vap2)
                            for pp_, (lh, rh) in enumerate(
                                    ((gth, aph), (gth, apl), (gtl, aph))):
                                mk = (lambda e, dm=dm, q=q, pp_=pp_, lh=lh,
                                      rh=rh, col=col: e.matmul(
                                    psxo[:, dm * 512:(dm + 1) * 512],
                                    lh[:, col:col + 128],
                                    rh[:, q * C + dm * 512:q * C + (dm + 1) * 512],
                                    start=(q == 0 and pp_ == 0), stop=(q == 7 and pp_ == 2)))
                                if q == 7 and pp_ == 2:
                                    m_xo_mm[(tb, dm)] = em.step("tensor", "T", mk)
                                else:
                                    em.run("tensor", mk)

                def partEv(tb):
                    em.wait("scalar", "S", m_ln2.get(tb - 1, 0))
                    for dm in range(4):
                        em.wait("scalar", "T", m_xo_mm[(tb, dm)])
                        mk_ = em.step("scalar", "S", lambda e, dm=dm: e.copy(
                            xob[:, dm * 512:(dm + 1) * 512],
                            psxo[:, dm * 512:(dm + 1) * 512]))
                        m_xo_ev[(tb, dm)] = ("S", mk_)

                nbv = [nbb, nbb2]
                invs2 = [sb2("invs2a", [128, 1]), sb2("invs2b", [128, 1])]
                m_invs = {}

                def partChain(tb):
                    p = tb % 2
                    em.wait("scalar", "S", m_xo_ev[(tb, 3)][1])
                    em.wait("vector", "S", m_xo_ev[(tb, 3)][1])
                    em.wait("scalar", "V", m_add.get(tb - 1, 0))
                    m_ln2[tb] = emit_ln(em, st2[p], xob[:, :], y2b[:, :], 128, C, eps5_t)
                    em.wait("vector", "S", m_ln2[tb])
                    em.wait("vector", "XA%d" % p, vx[tb])
                    em.wait("vector", "PK", pkh["v"])
                    em.wait("vector", "S", m_ln3.get(tb - 1, 0))
                    m_add[tb] = em.step("vector", "V", lambda e, p=p: e.tensor_add(
                        x2b[:, :], xbv[p][:, :], y2b[:, :]))
                    em.wait("gpsimd", "V", m_add[tb])
                    pkh["v"] = em.dma("gpsimd",
                                      x2buf[t0 + tb * 128:t0 + tb * 128 + 128, :],
                                      x2b[:], "PK")
                    dma_x2b(tb + 2)
                    em.wait("scalar", "V", m_add[tb])
                    m_ln3[tb] = emit_ln(em, st2[p], x2b[:, :], y2b[:, :], 128, C, eps5_t)
                    em.wait("vector", "S", m_ln3[tb])
                    em.wait("scalar", "V", m_nbb.get(tb - 1, 0))
                    m_rms[tb] = emit_ln(em, st2[p], y2b[:, :], y3b[:, :], 128, C, eps8_t,
                                        s_src=m_ln3[tb])
                    em.wait("vector", "S", m_rms[tb])
                    va1 = em.step("vector", "V", lambda e, p=p: e.tensor_reduce(
                        st2[p]["am"][:, :1], y3b[:, :], axis=AX.X, op=ALU.max,
                        apply_absolute_value=True))
                    em.wait("vector", "V", va1)
                    va2 = em.step("vector", "V", lambda e, p=p: e.reciprocal(
                        st2[p]["r0"][:, :1], st2[p]["am"][:, :1]))
                    em.wait("vector", "V", va2)
                    va3 = em.step("vector", "V", lambda e, p=p: e.tensor_scalar(
                        st2[p]["rstd"][:, :1], st2[p]["r0"][:, :1], 127.0, None, ALU.mult))
                    em.wait("vector", "V", va3)
                    em.wait("vector", "T", t_nT.get(tb - 2, 0))
                    m_nbb[tb] = 0
                    for c2 in (0, 1024):
                        vni = em.step("vector", "V", lambda e, p=p, c2=c2: e.tensor_scalar(
                            nib[:, :], y3b[:, c2:c2 + 1024],
                            st2[p]["rstd"][:, :1], None, ALU.mult))
                        em.wait("vector", "V", vni)
                        m_nbb[tb] = em.step("vector", "V", lambda e, p=p, c2=c2: e.tensor_copy(
                            nbv[p][:, c2:c2 + 1024], nib[:, :]))
                        em.wait("vector", "V", m_nbb[tb])
                    m_invs[tb] = em.step("vector", "V", lambda e, p=p: e.tensor_scalar(
                        invs2[p][:, :1], st2[p]["am"][:, :1], 1.0 / 127.0,
                        None, ALU.mult))
                def partTrs(tb):
                    p = tb % 2
                    em.wait("tensor", "V", m_invs[tb])
                    em.wait("tensor", "V", m_psr.get(tb - 1, 0))
                    vtr = em.step("tensor", "T", lambda e, p=p: e.transpose(
                        psr[:1, :128], invs2[p][:, :1], identf_t[:, :]))
                    em.wait("vector", "T", vtr)
                    m_psr[tb] = em.step("vector", "V", lambda e, tb=tb: e.tensor_copy(
                        srow[:1, tb * 128:(tb + 1) * 128], psr[:1, :128]))
                    em.wait("tensor", "V", m_nbb[tb])
                    carN['c'], _ = emit_trsN(em, [
                        ([(lambda e, b, c=8 * g + j, j=j, p=p: e.transpose(
                              b[:, j * 128:(j + 1) * 128],
                              nbv[p][:, c * 128:(c + 1) * 128], identb_t[:, :]))
                          for j in range(8)],
                         [lambda e, b, g=g, tb=tb: e.tensor_copy(
                             bass.AP(nT, (8 * g) * HALF + tb * 128,
                                     [[NF * HALF, 128], [HALF, 8], [1, 128]]),
                             bass.AP(b, 0, [[1024, 128], [128, 8], [1, 128]]))])
                        for g in range(2)], pstnA, pstnB, carry=carN['c'])
                    t_nT[tb] = em.val("T")

                for k in range(NTB + 2):
                    if 1 <= k <= NTB:
                        partEv(k - 1)
                    if k < NTB:
                        partA2b(k)
                    if 1 <= k <= NTB:
                        partChain(k - 1)
                    if k >= 2:
                        partTrs(k - 2)
                em.flush()
            gtstk.close()

            with nc.psum_tensor("psbc_h%d" % half, [128, 1024], F32) as psb_:
                em.run("tensor", lambda e: e.matmul(
                    psb_[:, 0:512], onesf_t[:1, :],
                    srow[:1, 0:512], start=True, stop=True))
                vt = em.step("tensor", "T", lambda e: e.matmul(
                    psb_[:, 512:1024], onesf_t[:1, :],
                    srow[:1, 512:1024], start=True, stop=True))
                em.wait("vector", "T", vt)
                em.run("vector", lambda e: e.tensor_copy(sbc[:, :], psb_[:, :]))
                em.flush()

            if dbg and half == 0:
                em.dma("gpsimd", d_nT[:, :], nT[:, :], "GP")
                em.dma("gpsimd", d_sbc[:, :], sbc[:, :], "GP")
                em.flush()

            with contextlib.ExitStack() as s3o:
                d_sb = s3o.enter_context(nc.sbuf_tensor("d_sb_h%d" % half, [128, 8 * C], F32))
                ws = contextlib.ExitStack()
                sb3 = lambda n, s, d=F32: ws.enter_context(
                    nc.sbuf_tensor("%s_h%d" % (n, half), s, d))
                stg = sb3("stg", [128, 3 * C])
                swg = sb3("swg", [128, 2 * C], BF16)
                swu = sb3("swu", [128, 2 * C], BF16)
                pth = sb3("pth", [128, NHB * HALF], BF16)
                ptl = sb3("ptl", [128, NHB * HALF], BF16)
                gsil = sb3("gsil", [128, 512])
                eg = sb3("eg", [128, 512])
                eu = sb3("eu", [128, 512])
                epb = sb3("epb", [128, 512])
                wstg = sb3("wstg", [128, 2 * 512])
                swd = sb3("swd", [128, 2 * 512], BF16)
                pst3 = contextlib.ExitStack()
                NPAIR = 3
                psg = [pst3.enter_context(nc.psum_tensor(
                    "psg%d_h%d" % (i, half), [128, 512], F32)) for i in range(NPAIR)]
                psu = [pst3.enter_context(nc.psum_tensor(
                    "psu%d_h%d" % (i, half), [128, 512], F32)) for i in range(NPAIR)]
                psdx = [pst3.enter_context(nc.psum_tensor(
                    "psdx%d_h%d" % (i, half), [128, 512], F32)) for i in range(8 - 2 * NPAIR)]
                psd = psg + psu + psdx

                em.fence_dma()
                sign_done = {}
                vepm = {}
                pair_T = {}
                pair_V = {}
                wsign = {}
                mmdone = {}
                mlast = {}
                evadd = {}

                def emit_up_load(lid):
                    hbg = lid // 2
                    if hbg >= H // 128:
                        return 0
                    wmat = wgTd if lid % 2 == 0 else wuTd
                    if lid >= 3:
                        em.wait("sync", "S", sign_done.get(lid - 3, 0))
                    return em.dma("sync", bass.AP(stg, (lid % 3) * C,
                                                  [[3 * C, 128], [1, C]]),
                                  wmat[hbg, :, :], "UP%d" % (lid % 3))
                up_ld = {}

                def emit_wd_load(wid):
                    if wid >= NSUP * 4 * NHB:
                        return 0
                    supw, rem = divmod(wid, 4 * NHB)
                    dmw, hcw = divmod(rem, NHB)
                    if wid >= 2:
                        em.wait("sync", "S", wsign.get(wid - 2, 0))
                    off = (supw * HS + hcw * 128) * C + dmw * 512
                    return em.dma("sync", wstg[:, (wid % 2) * 512:(wid % 2 + 1) * 512],
                                  bass.AP(wdTd, off, [[C, 128], [1, 512]]), "WD%d" % (wid % 2))
                wd_ld = {}

                for lid in range(2):
                    up_ld[lid] = emit_up_load(lid)

                def emit_up_sign(hbg2s):
                    if hbg2s >= NSUP * NHB:
                        return
                    l0s, l1s = 2 * hbg2s, 2 * hbg2s + 1
                    em.wait("scalar", "UP%d" % (l0s % 3), up_ld[l0s])
                    em.wait("scalar", "UP%d" % (l1s % 3), up_ld[l1s])
                    if hbg2s >= 2:
                        em.wait("scalar", "T", pair_T.get((hbg2s - 2, 1), 0))
                    sign_done[l0s] = em.step("scalar", "S",
                        lambda e, sl=(l0s % 3), so=(hbg2s % 2): e.activation(
                            swg[:, so * C:(so + 1) * C],
                            stg[:, sl * C:(sl + 1) * C], AF.Sign))
                    sign_done[l1s] = em.step("scalar", "S",
                        lambda e, sl=(l1s % 3), so=(hbg2s % 2): e.activation(
                            swu[:, so * C:(so + 1) * C],
                            stg[:, sl * C:(sl + 1) * C], AF.Sign))
                    up_ld[l0s + 2] = emit_up_load(l0s + 2)
                    up_ld[l1s + 2] = emit_up_load(l1s + 2)

                emit_up_sign(0)
                for sup in range(NSUP):
                    h0 = sup * HS
                    for hb in range(NHB):
                        hbg2 = sup * NHB + hb
                        l0, l1 = 2 * hbg2, 2 * hbg2 + 1
                        sdg = sign_done[l0]
                        sd = sign_done[l1]
                        em.wait("tensor", "S", sdg)
                        slot = hbg2 % 2
                        for tt in range(2):
                            gpair = hbg2 * 2 + tt
                            pr = gpair % NPAIR
                            if sup >= 1 and gpair - sup * NHB * 2 < NPAIR:
                                em.wait("tensor", "V", evadd[(sup * 4 - 1, NPAIR + pr)])
                            em.wait("tensor", "V", pair_V.get(gpair - NPAIR, 0))
                            for f in range(NF):
                                em.run("tensor", lambda e, f=f, pr=pr, tt=tt, slot=slot: e.matmul(
                                    psg[pr][:, :],
                                    swg[:, slot * C + f * 128:slot * C + (f + 1) * 128],
                                    nT[:, f * HALF + tt * 512:f * HALF + (tt + 1) * 512],
                                    start=(f == 0), stop=(f == NF - 1)))
                            if tt == 0:
                                em.wait("tensor", "S", sd)
                            for f in range(NF - 1):
                                em.run("tensor", lambda e, f=f, pr=pr, tt=tt, slot=slot: e.matmul(
                                    psu[pr][:, :],
                                    swu[:, slot * C + f * 128:slot * C + (f + 1) * 128],
                                    nT[:, f * HALF + tt * 512:f * HALF + (tt + 1) * 512],
                                    start=(f == 0), stop=False))
                            pair_T[(hbg2, tt)] = em.step("tensor", "T",
                                lambda e, pr=pr, tt=tt, slot=slot: e.matmul(
                                    psu[pr][:, :],
                                    swu[:, slot * C + (NF - 1) * 128:slot * C + NF * 128],
                                    nT[:, (NF - 1) * HALF + tt * 512:(NF - 1) * HALF + (tt + 1) * 512],
                                    start=False, stop=True))
                        emit_up_sign(hbg2 + 1)
                        for tt in range(2):
                            gpair = hbg2 * 2 + tt
                            pr = gpair % NPAIR
                            em.wait("vector", "T", pair_T[(hbg2, tt)])
                            va = em.step("vector", "V", lambda e, pr=pr, tt=tt: e.tensor_mul(
                                eg[:, :], psg[pr][:, :], sbc[:, tt * 512:(tt + 1) * 512]))
                            em.wait("scalar", "V", va)
                            vsl = em.step("scalar", "S", lambda e: e.activation(
                                gsil[:, :], eg[:, :], AF.Silu))
                            em.wait("vector", "V", vepm.get(gpair - 1, 0))
                            veu = em.step("vector", "V", lambda e, pr=pr, tt=tt: e.tensor_mul(
                                eu[:, :], psu[pr][:, :], sbc[:, tt * 512:(tt + 1) * 512]))
                            em.wait("vector", "S", vsl)
                            em.wait("vector", "V", veu)
                            em.wait("vector", "V", pair_V.get(gpair - 1, 0))
                            vep = em.step("vector", "V", lambda e: e.tensor_mul(
                                epb[:, :], gsil[:, :], eu[:, :]))
                            vepm[gpair] = vep
                            col = hb * HALF + tt * 512
                            em.wait("vector", "V", vep)
                            vph = em.step("vector", "V", lambda e, col=col: e.tensor_copy(
                                pth[:, col:col + 512], epb[:, :]))
                            em.wait("vector", "V", vph)
                            pair_V[gpair] = em.step("vector", "V",
                                lambda e, col=col: e.scalar_tensor_tensor(
                                    ptl[:, col:col + 512], epb[:, :], 1.0,
                                    pth[:, col:col + 512], ALU.mult, ALU.subtract))
                    if sup == 0:
                        for wid in range(2):
                            wd_ld[wid] = emit_wd_load(wid)

                    if dbg and half == 0 and sup == 0:
                        em.dma("gpsimd", d_pth[:, :], pth[:, :], "GP")
                        em.dma("gpsimd", d_ptl[:, :], ptl[:, :], "GP")
                        em.flush()

                    if True:
                        for dm in range(4):
                            dmg = sup * 4 + dm
                            for hc in range(NHB):
                                wid = dmg * NHB + hc
                                slot = wid % 2
                                em.wait("scalar", "WD%d" % (wid % 2), wd_ld[wid])
                                if wid >= 2:
                                    em.wait("scalar", "T", mmdone.get(wid - 2, 0))
                                wsign[wid] = em.step("scalar", "S",
                                    lambda e, slot=slot: e.activation(
                                        swd[:, slot * 512:(slot + 1) * 512],
                                        wstg[:, slot * 512:(slot + 1) * 512], AF.Sign))
                                wd_ld[wid + 2] = emit_wd_load(wid + 2)
                                em.wait("tensor", "S", wsign[wid])
                                tb_order = (6, 7, 0, 1, 2, 3, 4, 5)
                                for tb in tb_order:
                                    if hc == 0 and dmg >= 1:
                                        em.wait("tensor", "V", evadd.get((dmg - 1, tb), 0))
                                    if hc == 0 and dm == 0 and tb == 0:
                                        em.wait("tensor", "V",
                                                pair_V[(sup * NHB + NHB) * 2 - 1])
                                    for pi, pslab in enumerate((pth, ptl)):
                                        mk = (lambda e, hc=hc, tb=tb, pi=pi,
                                              pslab=pslab, slot=slot: e.matmul(
                                            psd[tb][:, :],
                                            pslab[:, hc * HALF + tb * 128:hc * HALF + (tb + 1) * 128],
                                            swd[:, slot * 512:(slot + 1) * 512],
                                            start=(hc == 0 and pi == 0),
                                            stop=(hc == NHB - 1 and pi == 1)))
                                        if pi == 1 and hc == NHB - 1:
                                            mlast[(dmg, tb)] = em.step("tensor", "T", mk)
                                        elif tb == 5 and pi == 1:
                                            mmdone[wid] = em.step("tensor", "T", mk)
                                        else:
                                            em.run("tensor", mk)
                                if hc == NHB - 1:
                                    mmdone[wid] = mlast[(dmg, 5)]
                            if dmg >= 4:
                                em.wait("vector", "V", evadd[(dmg - 4, NTB - 1)])
                            for tb in (6, 7, 0, 1, 2, 3, 4, 5):
                                em.wait("vector", "T", mlast[(dmg, tb)])
                                dcol = tb * C + dm * 512
                                if sup == 0:
                                    evadd[(dmg, tb)] = em.step("vector", "V",
                                        lambda e, tb=tb, dcol=dcol: e.tensor_copy(
                                            d_sb[:, dcol:dcol + 512], psd[tb][:, :]))
                                else:
                                    evadd[(dmg, tb)] = em.step("vector", "V",
                                        lambda e, tb=tb, dcol=dcol: e.tensor_tensor(
                                            d_sb[:, dcol:dcol + 512], psd[tb][:, :],
                                            d_sb[:, dcol:dcol + 512], ALU.add))

                em.flush()
                pst3.close()
                ws.close()
                with contextlib.ExitStack() as s4:
                    sb4 = lambda n, s, d=F32: s4.enter_context(
                        nc.sbuf_tensor("%s_h%d4" % (n, half), s, d))
                    xbl4 = [sb4("xbl4a", [128, C]), sb4("xbl4b", [128, C])]
                    l4v = [sb4("l4va", [128, C]), sb4("l4vb", [128, C])]
                    outb = sb4("outb", [128, C])
                    vout = 0
                    em.wait("sync", "PK", em.val("PK"))
                    em.fence_dma(engines=("vector", "scalar"))
                    vadd4 = {}
                    vx4 = {}
                    for tb in range(2):
                        vx4[tb] = em.dma("sync", xbl4[tb][:],
                                         x2buf[t0 + tb * 128:t0 + tb * 128 + 128, :],
                                         "XA%d" % (tb % 2))
                    for tb in range(NTB):
                        row0 = t0 + tb * 128
                        m4 = emit_ln(em, st2[tb % 2], d_sb[:, tb * C:(tb + 1) * C],
                                     l4v[tb % 2][:, :], 128, C, eps5_t)
                        em.wait("vector", "S", m4)
                        em.wait("vector", "XA%d" % (tb % 2), vx4[tb])
                        em.wait("vector", "OU", vout)
                        vo = em.step("vector", "V", lambda e, p=tb % 2: e.tensor_add(
                            outb[:, :], xbl4[p][:, :], l4v[p][:, :]))
                        vadd4[tb] = vo
                        em.wait("gpsimd", "V", vo)
                        vout = em.dma("gpsimd", ys[row0:row0 + 128, :], outb[:], "OU")
                        if tb + 2 < NTB:
                            em.wait("sync", "V", vo)
                            vx4[tb + 2] = em.dma(
                                "sync", xbl4[tb % 2][:],
                                x2buf[t0 + (tb + 2) * 128:t0 + (tb + 2) * 128 + 128, :],
                                "XA%d" % (tb % 2))
                    em.flush()

    em.wait("gpsimd", "OU", em.val("OU"))
    em.wait("sync", "OU", em.val("OU"))
    em.wait("gpsimd", "PK", em.val("PK"))
    em.wait("gpsimd", "GP", em.val("GP"))
    em.flush()
    gst.close()
    return nc


_host_cache = {}


def _host_consts():
    if _host_cache:
        return _host_cache
    identf = np.eye(128, dtype=np.float32)
    identb = np.eye(128).astype(ml_dtypes.bfloat16)
    onesf = np.ones((1, 128), dtype=np.float32)
    ll = np.arange(L)
    maskadd = np.where(ll[None, :] <= ll[:, None], 0.0, -1e9).astype(np.float32)
    binmask = (ll[None, :] <= ll[:, None]).astype(np.float32)
    j = np.arange(64, dtype=np.float64)
    inv_freq = 1.0 / (10000.0 ** (j * 2.0 / 128.0))
    t = np.arange(L, dtype=np.float64)
    ang = t[:, None] * inv_freq[None, :]
    cosv = np.cos(ang)
    sinv = np.sin(ang)
    ropeC = np.zeros((L, C), dtype=np.float32)
    ropeS = np.zeros((L, C), dtype=np.float32)
    for h in range(NH):
        ropeC[:, h * 128 + 0:h * 128 + 128:2] = cosv
        ropeC[:, h * 128 + 1:h * 128 + 128:2] = cosv
        ropeS[:, h * 128 + 0:h * 128 + 128:2] = -sinv
        ropeS[:, h * 128 + 1:h * 128 + 128:2] = sinv
    _host_cache.update(identf=identf, identb=identb, onesf=onesf,
                       maskadd=maskadd, binmask=binmask, ropeC=ropeC, ropeS=ropeS)
    return _host_cache


def kernel(L0_attn_x, L0_attn_latent_queries, L0_attn_Wk, L0_attn_Wv,
           L0_attn_Wg, L0_attn_Wp, L0_ffn_W_g, L0_ffn_W_u, L0_ffn_W_d):
    x = np.ascontiguousarray(np.asarray(L0_attn_x, dtype=np.float32))
    latq = np.asarray(L0_attn_latent_queries, dtype=np.float32)
    wk = np.ascontiguousarray(np.asarray(L0_attn_Wk, dtype=np.float32))
    wv = np.ascontiguousarray(np.asarray(L0_attn_Wv, dtype=np.float32))
    wg = np.ascontiguousarray(np.asarray(L0_attn_Wg, dtype=np.float32))
    wp = np.ascontiguousarray(np.asarray(L0_attn_Wp, dtype=np.float32))
    def tile_up(w):
        wt = np.asarray(w, dtype=np.float32).T
        t = wt.reshape(NF, 128, H // 128, 128)
        return np.ascontiguousarray(t.transpose(2, 1, 0, 3).reshape(H // 128, 128, C))
    wgT = tile_up(L0_ffn_W_g)
    wuT = tile_up(L0_ffn_W_u)
    wdT = np.ascontiguousarray(np.asarray(L0_ffn_W_d, dtype=np.float32).T)
    latqT = np.ascontiguousarray(latq[0].transpose(2, 1, 0).reshape(128, NH * L))
    cst = _host_consts()

    nc = build_program()
    in_maps = []
    for c in range(8):
        b = c // 4
        r0 = (c % 4) * TOK
        in_maps.append(dict(
            xs=np.ascontiguousarray(x[b, r0:r0 + TOK, :]),
            xh=np.ascontiguousarray(x[b, 0:L, :]),
            latqT=latqT, wk=wk, wv=wv, wp=wp, wga=wg,
            wgT=wgT, wuT=wuT, wdT=wdT,
            identf=cst["identf"], identb=cst["identb"], onesf=cst["onesf"],
            maskadd=cst["maskadd"], binmask=cst["binmask"],
            ropeC=cst["ropeC"], ropeS=cst["ropeS"]))
    res = run_bass_kernel_spmd(nc, in_maps, core_ids=list(range(8)))
    out = np.empty((2, 8192, C), dtype=np.float32)
    for c in range(8):
        b = c // 4
        r0 = (c % 4) * TOK
        out[b, r0:r0 + TOK, :] = res.results[c]["ys"]
    return out


if __name__ == "__main__":
    nc = build_program()
    print("program built ok")


# version 25
# speedup vs baseline: 1.1923x; 1.1923x over previous
import contextlib
import numpy as np
import ml_dtypes
import concourse.bass as bass
import concourse.mybir as mybir
from concourse.bass_utils import run_bass_kernel_spmd

F32 = mybir.dt.float32
BF16 = mybir.dt.bfloat16
I32 = mybir.dt.int32
AF = mybir.ActivationFunctionType
ALU = mybir.AluOpType
AX = mybir.AxisListType

C = 2048
H = 8192
NH = 16
L = 64
TOK = 2048
HALF = 1024
NTB = 8
NF = C // 128
SCALE = float(1.0 / np.sqrt(128.0))
EPS_LN = 1e-5
EPS_RMS = 1e-8
HS = 1024
NSUP = H // HS
NHB = HS // 128
ENGINES = ("sync", "scalar", "vector", "tensor", "gpsimd")


class EM:
    def __init__(self, nc, sems):
        self.nc = nc
        self.sems = sems
        self.q = {e: [] for e in ENGINES}
        self.cnt = {k: 0 for k in sems}

    def on(self, eng, fn):
        self.q[eng].append(fn)

    def val(self, name):
        return self.cnt[name]

    def wait(self, eng, name, v):
        if v <= 0:
            return
        s = self.sems[name]
        self.on(eng, lambda e: e.wait_ge(s, v))

    def dma(self, eng, dst, src, name="LD", **kw):
        self.cnt[name] += 16
        v = self.cnt[name]
        s = self.sems[name]
        self.on(eng, lambda e: e.dma_start(dst, src, **kw).then_inc(s, 16))
        return v

    def step(self, eng, name, build):
        self.cnt[name] += 1
        v = self.cnt[name]
        s = self.sems[name]
        self.on(eng, lambda e: build(e).then_inc(s, 1))
        return v

    def run(self, eng, build):
        self.on(eng, lambda e: build(e))

    def fence_dma(self, engines=("sync", "vector", "scalar", "gpsimd")):
        for nm in ("PK", "OU", "GP"):
            v = self.cnt[nm]
            if v:
                for eng in engines:
                    self.wait(eng, nm, v)

    def flush(self):
        q = self.q
        self.q = {e: [] for e in ENGINES}
        if not any(q.values()):
            return
        with self.nc.Block(no_gpsimd_drain=True) as blk:
            for eng in ENGINES:
                fns = q[eng]
                if fns:
                    def make(fl):
                        def body(e):
                            for f in fl:
                                f(e)
                        return body
                    getattr(blk, eng)(make(fns))


def emit_trsN(em, groups, psA, psB, carry=None):
    ca, cb = carry if carry else (0, 0)
    vcop = {-2: ca, -1: cb}
    for i, (tfns, vfns) in enumerate(groups):
        bank = psA if i % 2 == 0 else psB
        prev = vcop.get(i - 2, 0)
        if prev:
            em.wait("tensor", "V", prev)
        vt = 0
        for j, tf in enumerate(tfns):
            if j == len(tfns) - 1:
                vt = em.step("tensor", "T", lambda e, f=tf, b=bank: f(e, b))
            else:
                em.run("tensor", lambda e, f=tf, b=bank: f(e, b))
        em.wait("vector", "T", vt)
        mk = 0
        for j, vf in enumerate(vfns):
            if j > 0:
                em.wait("vector", "V", mk)
            mk = em.step("vector", "V", lambda e, f=vf, b=bank: f(e, b))
        vcop[i] = mk
    n = len(groups)
    lastA = vcop[n - 2] if n % 2 == 0 else vcop[n - 1]
    lastB = vcop[n - 1] if n % 2 == 0 else vcop[n - 2]
    return (lastA, lastB), vcop[n - 1]


def emit_trs(em, items, psA, psB, carry=None):
    ca, cb = carry if carry else (0, 0)
    vcop = {-2: ca, -1: cb}
    for i, (tfn, vfn) in enumerate(items):
        bank = psA if i % 2 == 0 else psB
        prev = vcop.get(i - 2, 0)
        if prev:
            em.wait("tensor", "V", prev)
        vt = em.step("tensor", "T", lambda e, f=tfn, b=bank: f(e, b))
        em.wait("vector", "T", vt)
        if isinstance(vfn, (list, tuple)):
            mk = 0
            for j, vf in enumerate(vfn):
                if j > 0:
                    em.wait("vector", "V", mk)
                mk = em.step("vector", "V", lambda e, f=vf, b=bank: f(e, b))
            vcop[i] = mk
        else:
            vcop[i] = em.step("vector", "V", lambda e, f=vfn, b=bank: f(e, b))
    n = len(items)
    lastA = vcop[n - 2] if n % 2 == 0 else vcop[n - 1]
    lastB = vcop[n - 1] if n % 2 == 0 else vcop[n - 2]
    return (lastA, lastB), vcop[n - 1]


def emit_ln(em, st, src, dst, nrows, ncols, eps_ap, s_src=0):
    r = slice(0, nrows)
    if s_src:
        em.wait("scalar", "S", s_src)
    vs = em.step("scalar", "S", lambda e: e.activation(
        st["trash"][r, :ncols], src, AF.Square, accum_out=st["sq"][r, :1]))
    v1 = em.step("vector", "V", lambda e: e.reduce_sum(st["sm"][r, :1], src, axis=AX.X))
    inv_n = 1.0 / ncols
    em.wait("scalar", "V", v1)
    s1 = em.step("scalar", "S", lambda e: e.mul(st["mm"][r, :1], st["sm"][r, :1], inv_n))
    em.wait("scalar", "S", vs)
    s2 = em.step("scalar", "S", lambda e: e.mul(st["vv"][r, :1], st["sq"][r, :1], inv_n))
    em.wait("scalar", "S", s1)
    s3 = em.step("scalar", "S", lambda e: e.activation(
        st["r0"][r, :1], st["mm"][r, :1], AF.Square))
    em.wait("scalar", "S", s3)
    s5 = em.step("scalar", "S", lambda e: e.activation(
        st["sq"][r, :1], st["r0"][r, :1], AF.Identity, bias=st["vv"][r, :1],
        scale=-1.0))
    em.wait("scalar", "S", s5)
    s6 = em.step("scalar", "S", lambda e: e.activation(
        st["r0"][r, :1], st["sq"][r, :1], AF.Sqrt, bias=eps_ap[r, :1]))
    em.wait("vector", "S", s6)
    v2 = em.step("vector", "V", lambda e: e.reciprocal(st["rstd"][r, :1], st["r0"][r, :1]))
    em.wait("scalar", "V", v2)
    s7 = em.step("scalar", "S", lambda e: e.mul(st["mm"][r, :1], st["mm"][r, :1], -1.0))
    em.wait("scalar", "S", s7)
    s8 = em.step("scalar", "S", lambda e: e.mul(st["nbias"][r, :1], st["rstd"][r, :1],
                                                st["mm"][r, :1]))
    em.wait("scalar", "S", s8)
    return em.step("scalar", "S", lambda e: e.activation(
        dst, src, AF.Identity, bias=st["nbias"][r, :1], scale=st["rstd"][r, :1]))


def build_program(dbg=False):
    nc = bass.Bass("TRN2", target_bir_lowering=False)
    DK = "ExternalOutput" if dbg else "Internal"

    xs = nc.dram_tensor("xs", [TOK, C], F32, kind="ExternalInput")
    xhd = nc.dram_tensor("xh", [L, C], F32, kind="ExternalInput")
    latqT = nc.dram_tensor("latqT", [128, NH * L], F32, kind="ExternalInput")
    wkd = nc.dram_tensor("wk", [C, C], F32, kind="ExternalInput")
    wvd = nc.dram_tensor("wv", [C, C], F32, kind="ExternalInput")
    wpd = nc.dram_tensor("wp", [C, C], F32, kind="ExternalInput")
    wgad = nc.dram_tensor("wga", [C, NH * L], F32, kind="ExternalInput")
    wgTd = nc.dram_tensor("wgT", [H // 128, 128, C], F32, kind="ExternalInput")
    wuTd = nc.dram_tensor("wuT", [H // 128, 128, C], F32, kind="ExternalInput")
    wdTd = nc.dram_tensor("wdT", [H, C], F32, kind="ExternalInput")
    identf = nc.dram_tensor("identf", [128, 128], F32, kind="ExternalInput")
    identb = nc.dram_tensor("identb", [128, 128], BF16, kind="ExternalInput")
    onesf = nc.dram_tensor("onesf", [1, 128], F32, kind="ExternalInput")
    maskadd = nc.dram_tensor("maskadd", [L, L], F32, kind="ExternalInput")
    binmask = nc.dram_tensor("binmask", [L, L], F32, kind="ExternalInput")
    ropeC = nc.dram_tensor("ropeC", [L, C], F32, kind="ExternalInput")
    ropeS = nc.dram_tensor("ropeS", [L, C], F32, kind="ExternalInput")

    ys = nc.dram_tensor("ys", [TOK, C], F32, kind="ExternalOutput")

    apbuf_hi = nc.dram_tensor("apbuf_hi", [NH * L, C], BF16, kind=DK)
    apbuf_lo = nc.dram_tensor("apbuf_lo", [NH * L, C], BF16, kind=DK)
    x2buf = nc.dram_tensor("x2buf", [TOK, C], F32, kind=DK)
    wgac = nc.dram_tensor("wgac", [2, C, 1024], BF16, kind="Internal")
    if dbg:
        d_y1h = nc.dram_tensor("d_y1h", [L, C], F32, kind="ExternalOutput")
        d_k = nc.dram_tensor("d_k", [L, C], F32, kind="ExternalOutput")
        d_attnT = nc.dram_tensor("d_attnT", [128, NH * L], F32, kind="ExternalOutput")
        d_gth = nc.dram_tensor("d_gth", [128, 8 * HALF], BF16, kind="ExternalOutput")
        d_nT = nc.dram_tensor("d_nT", [128, NF * HALF], BF16, kind="ExternalOutput")
        d_sbc = nc.dram_tensor("d_sbc", [128, HALF], F32, kind="ExternalOutput")
        d_pth = nc.dram_tensor("d_pth", [128, NHB * HALF], BF16, kind="ExternalOutput")
        d_ptl = nc.dram_tensor("d_ptl", [128, NHB * HALF], BF16, kind="ExternalOutput")

    sems = {k: nc.alloc_semaphore(name="sem_" + k)
            for k in ("LD", "GP", "T", "V", "S", "XA0", "XA1", "WG0", "WG1",
                      "UP0", "UP1", "UP2", "WD0", "WD1", "WP0", "WP1",
                      "KV0", "KV1", "PK", "OU", "GC")}
    em = EM(nc, sems)

    gst = contextlib.ExitStack()
    gsb = lambda n, s, d=F32: gst.enter_context(nc.sbuf_tensor(n, s, d))

    identf_t = gsb("identf_t", [128, 128])
    identb_t = gsb("identb_t", [128, 128], BF16)
    onesf_t = gsb("onesf_t", [1, 128])
    st = {k: gsb("st_" + k, [128, 1]) for k in
          ("sq", "sm", "mm", "vv", "r0", "rstd", "nbias", "am", "invs")}
    st["trash"] = gsb("st_trash", [128, C])
    eps5_t = gsb("eps5_t", [128, 1])
    eps8_t = gsb("eps8_t", [128, 1])
    stB = {k: gsb("stB_" + k, [128, 1]) for k in
           ("sq", "sm", "mm", "vv", "r0", "rstd", "nbias", "am", "invs")}
    stB["trash"] = st["trash"]
    st2 = [st, stB]

    v0 = None
    for dst, srck in [(identf_t[:], identf[:, :]), (identb_t[:], identb[:, :]),
                      (onesf_t[:], onesf[:, :])]:
        v0 = em.dma("sync", dst, srck, "LD")
    em.wait("sync", "LD", v0)
    em.run("vector", lambda e: e.memset(eps5_t[:, :1], EPS_LN))
    em.run("vector", lambda e: e.memset(eps8_t[:, :1], EPS_RMS))
    em.flush()

    with contextlib.ExitStack() as ps:
        psb = lambda n, s, d=F32: ps.enter_context(nc.sbuf_tensor(n, s, d))
        pps = lambda n, s, d=F32: ps.enter_context(nc.psum_tensor(n, s, d))
        maskadd_t = psb("maskadd_t", [L, L])
        binmask_t = psb("binmask_t", [L, L])
        ropeC_t = psb("ropeC_t", [L, C])
        ropeS_t = psb("ropeS_t", [L, C])
        xh_t = psb("xh_t", [L, C])
        y1h = psb("y1h", [L, C])
        y1hT = psb("y1hT", [128, NF * L])
        kbuf = psb("kbuf", [L, C])
        vbuf = psb("vbuf", [L, C])
        ksw = psb("ksw", [L, C])
        ktmp = psb("ktmp", [L, C])
        kT = psb("kT", [128, NH * L])
        qT = psb("qT", [128, NH * L])
        smb = psb("smb", [L, L])
        wb = psb("wb", [L, L])
        wTs = psb("wTs", [L, NH * L])
        attnT = psb("attnT", [128, NH * L])
        wchunk = psb("wchunk", [128, 2 * C])
        aph = psb("aph", [128, 8 * C], BF16)
        apl = psb("apl", [128, 8 * C], BF16)
        pbig = pps("pbig", [L, C])
        trstk = contextlib.ExitStack()
        pstA = trstk.enter_context(nc.psum_tensor("pstA", [128, 512], F32))
        pstB = trstk.enter_context(nc.psum_tensor("pstB", [128, 512], F32))

        for dst, srck in [(maskadd_t[:], maskadd[:, :]), (binmask_t[:], binmask[:, :]),
                          (ropeC_t[:], ropeC[:, :]), (ropeS_t[:], ropeS[:, :])]:
            em.dma("sync", dst, srck, "LD")
        em.dma("sync", xh_t[:], xhd[:, :], "LD")
        vl = em.dma("sync", qT[:, :], latqT[:, :], "LD")
        em.wait("scalar", "LD", vl)
        em.wait("vector", "LD", vl)
        mln1h = emit_ln(em, st, xh_t[:L, :], y1h[:L, :], L, C, eps5_t)
        em.wait("tensor", "S", mln1h)

        carP, vy1t = emit_trs(em, [
            (lambda e, b, c=c: e.transpose(
                b[:, :64], y1h[:L, c * 128:(c + 1) * 128], identf_t[:L, :L]),
             lambda e, b, c=c: e.tensor_copy(y1hT[:, c * L:(c + 1) * L], b[:, :64]))
            for c in range(NF)], pstA, pstB)
        em.wait("tensor", "V", vy1t)

        kv_T = {}
        for wi, (wmat, dst) in enumerate(((wkd, kbuf), (wvd, vbuf))):
            for f in range(NF):
                gi = wi * NF + f
                slot = (gi % 2) * C
                if gi >= 2:
                    em.wait("sync", "T", kv_T[gi - 2])
                vw = em.dma("sync", wchunk[:, slot:slot + C],
                            wmat[f * 128:(f + 1) * 128, :], "KV%d" % (gi % 2))
                em.wait("tensor", "KV%d" % (gi % 2), vw)
                for ot in range(4):
                    mk = (lambda e, f=f, ot=ot, slot=slot: e.matmul(
                        pbig[:L, ot * 512:(ot + 1) * 512],
                        y1hT[:, f * L:(f + 1) * L],
                        wchunk[:, slot + ot * 512:slot + (ot + 1) * 512],
                        start=(f == 0), stop=(f == NF - 1)))
                    if ot == 3:
                        kv_T[gi] = em.step("tensor", "T", mk)
                    else:
                        em.run("tensor", mk)
            em.wait("vector", "T", kv_T[wi * NF + NF - 1])
            vkv = em.step("vector", "V", lambda e, dst=dst: e.tensor_copy(
                dst[:L, :], pbig[:L, :]))
            em.wait("tensor", "V", vkv)
            if wi == 0:
                vkv_k = vkv

        em.wait("vector", "V", vkv_k)
        em.run("vector", lambda e: e.tensor_copy(
            bass.AP(ksw, 0, [[C, L], [2, C // 2]]),
            bass.AP(kbuf, 1, [[C, L], [2, C // 2]])))
        vr1 = em.step("vector", "V", lambda e: e.tensor_copy(
            bass.AP(ksw, 1, [[C, L], [2, C // 2]]),
            bass.AP(kbuf, 0, [[C, L], [2, C // 2]])))
        em.run("vector", lambda e: e.tensor_mul(ktmp[:L, :], kbuf[:L, :], ropeC_t[:L, :]))
        em.wait("vector", "V", vr1)
        vr2 = em.step("vector", "V", lambda e: e.tensor_mul(
            ksw[:L, :], ksw[:L, :], ropeS_t[:L, :]))
        em.wait("vector", "V", vr2)
        vrk = em.step("vector", "V", lambda e: e.tensor_add(
            kbuf[:L, :], ktmp[:L, :], ksw[:L, :]))
        em.wait("tensor", "V", vrk)

        carP, vkt = emit_trs(em, [
            (lambda e, b, c=c: e.transpose(
                b[:, :64], kbuf[:L, c * 128:(c + 1) * 128], identf_t[:L, :L]),
             lambda e, b, c=c: e.tensor_copy(kT[:, c * L:(c + 1) * L], b[:, :64]))
            for c in range(NF)], pstA, pstB, carry=carP)
        em.flush()

        trstk.close()
        hstk = contextlib.ExitStack()
        pss = hstk.enter_context(nc.psum_tensor("pss", [L, 512], F32))
        psa = hstk.enter_context(nc.psum_tensor("psa", [128, 512], F32))
        pss2 = hstk.enter_context(nc.psum_tensor("pss2", [L, 512], F32))
        psa2 = hstk.enter_context(nc.psum_tensor("psa2", [128, 512], F32))

        m_sc = {}
        m_at = {}
        t_sc = {}
        vs_prev = 0
        em.wait("tensor", "V", vkt)
        em.wait("tensor", "LD", vl)
        for h in range(NH):
            hsl = slice(h * L, (h + 1) * L)
            pb = pss if h % 2 == 0 else pss2
            pa = psa if h % 2 == 0 else psa2
            if h >= 2:
                em.wait("tensor", "V", m_sc[h - 2])
            vt = em.step("tensor", "T", lambda e, hsl=hsl, pb=pb: e.matmul(
                pb[:L, :L], qT[:, hsl], kT[:, hsl], start=True, stop=True))
            em.wait("vector", "T", vt)
            em.wait("vector", "S", vs_prev)
            vsm = em.step("vector", "V", lambda e, pb=pb: e.scalar_tensor_tensor(
                smb[:L, :L], pb[:L, :L], SCALE, maskadd_t[:L, :L], ALU.mult, ALU.add))
            em.wait("vector", "V", vsm)
            vv = em.step("vector", "V", lambda e: e.reduce_max(
                st["mm"][:L, :1], smb[:L, :L], axis=AX.X, negate=True))
            em.wait("scalar", "V", vv)
            vs = em.step("scalar", "S", lambda e: e.activation(
                wb[:L, :L], smb[:L, :L], AF.Exp, bias=st["mm"][:L, :1]))
            vs_prev = vs
            em.wait("vector", "S", vs)
            vbm = em.step("vector", "V", lambda e: e.tensor_mul(
                wb[:L, :L], wb[:L, :L], binmask_t[:L, :L]))
            em.wait("vector", "V", vbm)
            vrs = em.step("vector", "V", lambda e: e.reduce_sum(
                st["sm"][:L, :1], wb[:L, :L], axis=AX.X))
            em.wait("vector", "V", vrs)
            vrc = em.step("vector", "V", lambda e: e.reciprocal(
                st["r0"][:L, :1], st["sm"][:L, :1]))
            em.wait("vector", "V", vrc)
            vv2 = em.step("vector", "V", lambda e: e.tensor_scalar(
                wb[:L, :L], wb[:L, :L], st["r0"][:L, :1], None, ALU.mult))
            em.wait("tensor", "V", vv2)
            vt2 = em.step("tensor", "T", lambda e, pb=pb: e.transpose(
                pb[:L, 64:64 + L], wb[:L, :L], identf_t[:L, :L]))
            t_sc[h] = vt2
            em.wait("vector", "T", vt2)
            m_sc[h] = em.step("vector", "V", lambda e, hsl=hsl, pb=pb: e.tensor_copy(
                wTs[:L, hsl], pb[:L, 64:64 + L]))
            em.wait("tensor", "V", m_sc[h])
            if h >= 2:
                em.wait("tensor", "V", m_at[h - 2])
            vt3 = em.step("tensor", "T", lambda e, h=h, hsl=hsl, pa=pa: e.matmul(
                pa[:, :L], vbuf[:L, h * 128:(h + 1) * 128], wTs[:L, hsl],
                start=True, stop=True))
            em.wait("vector", "T", vt3)
            m_at[h] = em.step("vector", "V", lambda e, hsl=hsl, pa=pa: e.tensor_copy(
                attnT[:, hsl], pa[:, :L]))
        em.wait("tensor", "V", m_at[NH - 1])

        hstk.close()
        pbig2 = pps("pbig2", [L, C])
        pbv = [pbig, pbig2]
        m_ap = {}
        s_ap = {}
        wp_ld = {}
        for h in range(NH):
            slot = (h % 2) * C
            if h >= 2:
                em.wait("sync", "T", tconsumed[h - 2]) if False else None
            wp_ld[h] = em.dma("sync", wchunk[:, slot:slot + C],
                              wpd[h * 128:(h + 1) * 128, :], "WP%d" % (h % 2))
            if h == 1:
                break
        for h in range(NH):
            slot = (h % 2) * C
            em.wait("tensor", "WP%d" % (h % 2), wp_ld[h])
            if h >= 2:
                em.wait("tensor", "V", m_ap[h - 2])
                em.wait("tensor", "S", s_ap[h - 2])
            prow = (h % 2) * 64
            pcol = (h // 2) * C
            vt = None
            pb_ = pbv[h % 2]
            for dm in range(4):
                mk = (lambda e, h=h, dm=dm, slot=slot, pb_=pb_: e.matmul(
                    pb_[:L, dm * 512:(dm + 1) * 512],
                    attnT[:, h * L:(h + 1) * L],
                    wchunk[:, slot + dm * 512:slot + (dm + 1) * 512],
                    start=True, stop=True))
                if dm == 3:
                    vt = em.step("tensor", "T", mk)
                else:
                    em.run("tensor", mk)
            if h + 2 < NH:
                em.wait("sync", "T", vt)
                wp_ld[h + 2] = em.dma("sync", wchunk[:, ((h + 2) % 2) * C:((h + 2) % 2) * C + C],
                                      wpd[(h + 2) * 128:(h + 3) * 128, :], "WP%d" % (h % 2))
            em.wait("scalar", "T", vt)
            s_ap[h] = em.step("scalar", "S", lambda e, prow=prow, pcol=pcol, pb_=pb_: e.copy(
                aph[prow:prow + L, pcol:pcol + C], pb_[:L, :]))
            em.wait("vector", "S", s_ap[h])
            em.wait("vector", "T", vt)
            m_ap[h] = em.step("vector", "V", lambda e, prow=prow, pcol=pcol, pb_=pb_: e.scalar_tensor_tensor(
                apl[prow:prow + L, pcol:pcol + C], pb_[:L, :], 1.0,
                aph[prow:prow + L, pcol:pcol + C], ALU.mult, ALU.subtract))
        for j in range(8):
            em.wait("gpsimd", "V", m_ap[2 * j + 1])
            em.wait("gpsimd", "S", s_ap[2 * j + 1])
            dj = [[C, 128], [1, C]]
            em.dma("gpsimd", bass.AP(apbuf_hi, j * 128 * C, dj),
                   bass.AP(aph, j * C, [[8 * C, 128], [1, C]]), "GP")
            em.dma("gpsimd", bass.AP(apbuf_lo, j * 128 * C, dj),
                   bass.AP(apl, j * C, [[8 * C, 128], [1, C]]), "GP")
        em.flush()

    out_dma_total = 0
    for half in range(2):
        t0 = half * HALF
        with contextlib.ExitStack() as hst:
            hsb = lambda n, s, d=F32: hst.enter_context(
                nc.sbuf_tensor("%s_h%d" % (n, half), s, d))
            nT = hsb("nT", [128, NF * HALF], BF16)
            sbc = hsb("sbc", [128, HALF])
            srow = hsb("srow", [1, HALF])
            gtstk = contextlib.ExitStack()
            gsb2 = lambda n, s, d=F32: gtstk.enter_context(
                nc.sbuf_tensor("%s_h%d" % (n, half), s, d))
            gth = gsb2("gth", [128, 8 * HALF], BF16)
            gtl = gsb2("gtl", [128, 8 * HALF], BF16)
            gth_ref = gth

            em.fence_dma()
            with contextlib.ExitStack() as s2:
                sb2 = lambda n, s, d=F32: s2.enter_context(
                    nc.sbuf_tensor("%s_h%da" % (n, half), s, d))
                pp2 = lambda n, s, d=F32: s2.enter_context(
                    nc.psum_tensor("%s_h%da" % (n, half), s, d))
                wgah = sb2("wgah", [128, NF * 1024], BF16)
                wgal = sb2("wgal", [128, NF * 1024], BF16)
                wstg = sb2("wstg", [128, 2 * 1024])
                xblk = sb2("xblk", [128, C])
                xblk2 = sb2("xblk2", [128, C])
                y1 = sb2("y1", [128, C])
                y1b = sb2("y1b", [128, C])
                y1Th = sb2("y1Th", [128, C], BF16)
                y1Tl = sb2("y1Tl", [128, C], BF16)
                ebuf = sb2("ebuf", [128, 1024])
                gbuf = sb2("gbuf", [128, 1024])
                red = sb2("red", [128, 16])
                pstA2 = pp2("pstA2", [128, 512])
                pstB2 = pp2("pstB2", [128, 512])
                pslg = pp2("pslg", [128, 1024])
                pslg2 = pp2("pslg2", [128, 1024])

                m_wsp = {}
                wga_ld = 0
                slab_d = [[16 * 1024, 128], [1024, 16], [1, 1024]]
                dram_d = [[1024, 128], [128 * 1024, 16], [1, 1024]]
                if half == 0:
                    for f in range(NF):
                        slot = (f % 2) * 1024
                        if f >= 2:
                            em.wait("sync", "V", m_wsp[f - 2])
                        vw = em.dma("sync", wstg[:, slot:slot + 1024],
                                    wgad[f * 128:(f + 1) * 128, :], "WG%d" % (f % 2))
                        em.wait("vector", "WG%d" % (f % 2), vw)
                        vwc = em.step("vector", "V", lambda e, f=f, slot=slot: e.tensor_copy(
                            wgah[:, f * 1024:(f + 1) * 1024], wstg[:, slot:slot + 1024]))
                        em.wait("vector", "V", vwc)
                        m_wsp[f] = em.step("vector", "V", lambda e, f=f, slot=slot: e.scalar_tensor_tensor(
                            wgal[:, f * 1024:(f + 1) * 1024], wstg[:, slot:slot + 1024], 1.0,
                            wgah[:, f * 1024:(f + 1) * 1024], ALU.mult, ALU.subtract))
                    em.wait("gpsimd", "V", m_wsp[NF - 1])
                    em.dma("gpsimd", bass.AP(wgac, 0, dram_d),
                           bass.AP(wgah, 0, slab_d), "GP")
                    em.dma("gpsimd", bass.AP(wgac, C * 1024, dram_d),
                           bass.AP(wgal, 0, slab_d), "GP")
                else:
                    wga_ld0 = em.dma("sync", bass.AP(wgah, 0, slab_d),
                                     bass.AP(wgac, 0, dram_d), "WG0")
                    wga_ld = em.dma("sync", bass.AP(wgal, 0, slab_d),
                                    bass.AP(wgac, C * 1024, dram_d), "WG1")
                    for f in range(NF):
                        m_wsp[f] = 0

                xbv = [xblk, xblk2]
                y1v = [y1, y1b]
                y1Thv = [y1Th, y1Th]
                y1Tlv = [y1Tl, y1Tl]
                ebv = [ebuf, ebuf]
                gbv = [gbuf, gbuf]
                redv = [red, red]
                pslgv = [pslg, pslg2]
                m_aff = {}; m_sp = {}; m_lg = {}; m_rm = {}; m_ex = {}
                m_gb = {}; t_y1 = {}; t_gt = {}
                vx = {}
                carH = {'c': None}

                def dma_x2a(tb):
                    if tb >= NTB:
                        return
                    p = tb % 2
                    if tb >= 2:
                        em.wait("sync", "S", m_aff[tb - 2])
                    vx[tb] = em.dma("sync", xbv[p][:],
                                    xs[t0 + tb * 128:t0 + tb * 128 + 128, :], "XA%d" % p)
                dma_x2a(0)
                dma_x2a(1)

                def partLN(tb):
                    if tb >= NTB:
                        return
                    p = tb % 2
                    em.wait("scalar", "XA%d" % p, vx[tb])
                    em.wait("vector", "XA%d" % p, vx[tb])
                    if tb >= 2:
                        em.wait("scalar", "T", t_y1.get(tb - 2, 0))
                    m_aff[tb] = emit_ln(em, st2[p], xbv[p][:, :], y1v[p][:, :], 128, C, eps5_t)
                    dma_x2a(tb + 2)

                def partTrs2a(tb):
                    p = tb % 2
                    em.wait("tensor", "S", m_aff[tb])
                    if tb >= 1:
                        em.wait("vector", "T", m_lg.get(tb - 1, 0))
                    carH['c'], m_sp[tb] = emit_trsN(em, [
                        ([(lambda e, b, c=4 * g + j, j=j, p=p: e.transpose(
                              b[:, j * 128:(j + 1) * 128],
                              y1v[p][:, c * 128:(c + 1) * 128], identf_t[:, :]))
                          for j in range(4)],
                         [lambda e, b, c0=g * 512, p=p: e.tensor_copy(
                             y1Thv[p][:, c0:c0 + 512], b[:, :512]),
                          lambda e, b, c0=g * 512, p=p: e.scalar_tensor_tensor(
                             y1Tlv[p][:, c0:c0 + 512], b[:, :512], 1.0,
                             y1Thv[p][:, c0:c0 + 512], ALU.mult, ALU.subtract)])
                        for g in range(4)], pstA2, pstB2, carry=carH['c'])
                    t_y1[tb] = em.val("T")
                def partA2(tb):
                    p = tb % 2
                    em.wait("tensor", "V", m_sp[tb])
                    if tb >= 2:
                        em.wait("tensor", "S", m_ex.get(tb - 2, 0))
                        em.wait("tensor", "V", m_rm.get(tb - 2, 0))
                    for ot in range(2):
                        for f in range(NF):
                            if tb == 0 and ot == 0:
                                if half == 0:
                                    em.wait("tensor", "V", m_wsp[f])
                                elif f == 0:
                                    em.wait("tensor", "WG0", wga_ld0)
                                    em.wait("tensor", "WG1", wga_ld)
                            for pp_, (lh, rh) in enumerate(
                                    ((y1Thv[p], wgah), (y1Thv[p], wgal), (y1Tlv[p], wgah))):
                                mk = (lambda e, ot=ot, f=f, pp_=pp_, lh=lh, rh=rh, p=p: e.matmul(
                                    pslgv[p][:, ot * 512:(ot + 1) * 512],
                                    lh[:, f * 128:(f + 1) * 128],
                                    rh[:, f * 1024 + ot * 512:f * 1024 + (ot + 1) * 512],
                                    start=(f == 0 and pp_ == 0),
                                    stop=(f == NF - 1 and pp_ == 2)))
                                if ot == 1 and f == NF - 1 and pp_ == 2:
                                    m_lg[tb] = em.step("tensor", "T", mk)
                                else:
                                    em.run("tensor", mk)

                def partBsm(tb):
                    p = tb % 2
                    em.wait("vector", "T", m_lg[tb])
                    em.wait("vector", "S", m_ex.get(tb - 1, 0))
                    m_rm[tb] = em.step("vector", "V", lambda e, p=p: e.reduce_max(
                        bass.AP(redv[p], 0, [[16, 128], [1, 16], [0, 1]]),
                        bass.AP(pslgv[p], 0, [[1024, 128], [64, 16], [1, 64]]),
                        axis=AX.X, negate=True))
                    em.wait("vector", "V", m_rm[tb])
                    vng = em.step("vector", "V", lambda e, p=p: e.tensor_scalar(
                        redv[p][:, :16], redv[p][:, :16], SCALE, None, ALU.mult))
                    em.wait("scalar", "V", vng)
                    em.wait("scalar", "V", m_gb.get(tb - 1, 0))
                    for g in range(16):
                        mk = (lambda e, g=g, p=p: e.activation(
                            ebv[p][:, g * 64:(g + 1) * 64], pslgv[p][:, g * 64:(g + 1) * 64],
                            AF.Exp, bias=redv[p][:, g:g + 1], scale=SCALE))
                        if g == 15:
                            m_ex[tb] = em.step("scalar", "S", mk)
                        else:
                            em.run("scalar", mk)
                    em.wait("vector", "S", m_ex[tb])
                    vsu = em.step("vector", "V", lambda e, p=p: e.reduce_sum(
                        bass.AP(redv[p], 0, [[16, 128], [1, 16], [0, 1]]),
                        bass.AP(ebv[p], 0, [[1024, 128], [64, 16], [1, 64]]), axis=AX.X))
                    em.wait("vector", "V", vsu)
                    vrc = em.step("vector", "V", lambda e, p=p: e.reciprocal(
                        redv[p][:, :16], redv[p][:, :16]))
                    em.wait("vector", "V", vrc)
                    if tb >= 1:
                        em.wait("vector", "T", t_gt.get(tb - 1, 0))
                    for g in range(16):
                        mk = (lambda e, g=g, p=p: e.tensor_scalar(
                            gbv[p][:, g * 64:(g + 1) * 64], ebv[p][:, g * 64:(g + 1) * 64],
                            redv[p][:, g:g + 1], None, ALU.mult))
                        if g == 15:
                            m_gb[tb] = em.step("vector", "V", mk)
                        else:
                            em.run("vector", mk)
                def partBgt(tb):
                    p = tb % 2
                    em.wait("tensor", "V", m_gb[tb])
                    def gt_dst(t, g, tb=tb):
                        return bass.AP(t, (4 * g) * HALF + tb * 128,
                                       [[8 * HALF, 128], [HALF, 4], [1, 128]])
                    bsrc = lambda b: bass.AP(b, 0, [[512, 128], [128, 4], [1, 128]])
                    carH['c'], _ = emit_trsN(em, [
                        ([(lambda e, b, q=4 * g + j, j=j, p=p: e.transpose(
                              b[:, j * 128:(j + 1) * 128],
                              gbv[p][:, q * 128:(q + 1) * 128], identf_t[:, :]))
                          for j in range(4)],
                         [lambda e, b, g=g, f_=gt_dst, s_=bsrc: e.tensor_copy(
                              f_(gth, g), s_(b)),
                          lambda e, b, g=g, f_=gt_dst, s_=bsrc: e.scalar_tensor_tensor(
                              f_(gtl, g), s_(b), 1.0, f_(gth, g),
                              ALU.mult, ALU.subtract)])
                        for g in range(2)], pstA2, pstB2, carry=carH['c'])
                    t_gt[tb] = em.val("T")

                partLN(0)
                for k in range(NTB + 1):
                    partLN(k + 1)
                    if k < NTB:
                        partTrs2a(k)
                    if k >= 1:
                        partBsm(k - 1)
                    if k < NTB:
                        partA2(k)
                    if k >= 1:
                        partBgt(k - 1)
                v2a_end = em.val("V")
                s2a_end = em.val("S")
                t2a_lg = m_lg[NTB - 1]
                s2a_aff = m_aff[NTB - 1]
                v2a_wsp = m_wsp.get(NF - 1, 0)

            if dbg and half == 0:
                em.dma("gpsimd", d_gth[:, :], gth_ref[:, :], "GP")
                em.flush()

            with contextlib.ExitStack() as s2:
                sb2 = lambda n, s, d=F32: s2.enter_context(
                    nc.sbuf_tensor("%s_h%db" % (n, half), s, d))
                pp2 = lambda n, s, d=F32: s2.enter_context(
                    nc.psum_tensor("%s_h%db" % (n, half), s, d))
                aph = sb2("aph", [128, 8 * C], BF16)
                apl = sb2("apl", [128, 8 * C], BF16)
                xblk = sb2("xblk", [128, C])
                xblk2 = sb2("xblk2", [128, C])
                xob = sb2("xob", [128, C])
                x2b = sb2("x2b", [128, C])
                y2b = sb2("y2b", [128, C])
                y3b = sb2("y3b", [128, C])
                nib = sb2("nib", [128, 1024], I32)
                nbb = sb2("nbb", [128, C], BF16)
                nbb2 = sb2("nbb2", [128, C], BF16)
                psxo = pp2("psxo", [128, C])
                pstnA = pp2("pstnA", [128, 1024], BF16)
                pstnB = pp2("pstnB", [128, 1024], BF16)
                psr = pp2("psr", [128, 512])

                dsta = [[8 * C, 128], [C, 4], [1, C]]
                srca = [[C, 128], [128 * C, 4], [1, C]]
                em.wait("sync", "GP", em.val("GP"))
                em.wait("sync", "T", t2a_lg)
                em.wait("sync", "V", v2a_wsp)
                em.dma("sync", bass.AP(aph, 0, dsta), bass.AP(apbuf_hi, 0, srca), "LD")
                vap1 = em.dma("sync", bass.AP(apl, 0, dsta), bass.AP(apbuf_lo, 0, srca), "LD")
                em.dma("sync", bass.AP(aph, 4 * C, dsta),
                       bass.AP(apbuf_hi, 4 * 128 * C, srca), "KV0")
                vap2 = em.dma("sync", bass.AP(apl, 4 * C, dsta),
                              bass.AP(apbuf_lo, 4 * 128 * C, srca), "KV0")

                xbv = [xblk, xblk2]
                pkh = {'v': 0}
                m_add = {}; m_ln2 = {}; m_ln3 = {}; m_rms = {}
                m_xo_mm = {}; m_xo_ev = {}; m_nbb = {}; m_psr = {}
                t_nT = {}
                vx = {}
                carN = {'c': None}

                def dma_x2b(tb):
                    if tb >= NTB:
                        return
                    p = tb % 2
                    if tb < 2:
                        em.wait("sync", "S", s2a_aff)
                        em.wait("sync", "V", v2a_wsp)
                    if tb >= 2:
                        em.wait("sync", "V", m_add.get(tb - 2, 0))
                    vx[tb] = em.dma("sync", xbv[p][:],
                                    xs[t0 + tb * 128:t0 + tb * 128 + 128, :], "XA%d" % p)
                dma_x2b(0)
                dma_x2b(1)

                def partA2b(tb):
                    p = tb % 2
                    for dm in range(4):
                        if tb >= 1:
                            nm_, mk_ = m_xo_ev[(tb - 1, dm)]
                            em.wait("tensor", nm_, mk_)
                        for q in range(8):
                            col = q * HALF + tb * 128
                            if tb == 0 and dm == 0:
                                if q == 0:
                                    em.wait("tensor", "V", v2a_end)
                                    em.wait("tensor", "S", s2a_end)
                                    em.wait("tensor", "LD", vap1)
                                elif q == 4:
                                    em.wait("tensor", "KV0", vap2)
                            for pp_, (lh, rh) in enumerate(
                                    ((gth, aph), (gth, apl), (gtl, aph))):
                                mk = (lambda e, dm=dm, q=q, pp_=pp_, lh=lh,
                                      rh=rh, col=col: e.matmul(
                                    psxo[:, dm * 512:(dm + 1) * 512],
                                    lh[:, col:col + 128],
                                    rh[:, q * C + dm * 512:q * C + (dm + 1) * 512],
                                    start=(q == 0 and pp_ == 0), stop=(q == 7 and pp_ == 2)))
                                if q == 7 and pp_ == 2:
                                    m_xo_mm[(tb, dm)] = em.step("tensor", "T", mk)
                                else:
                                    em.run("tensor", mk)

                def partEv(tb):
                    em.wait("scalar", "S", m_ln2.get(tb - 1, 0))
                    for dm in range(4):
                        em.wait("scalar", "T", m_xo_mm[(tb, dm)])
                        mk_ = em.step("scalar", "S", lambda e, dm=dm: e.copy(
                            xob[:, dm * 512:(dm + 1) * 512],
                            psxo[:, dm * 512:(dm + 1) * 512]))
                        m_xo_ev[(tb, dm)] = ("S", mk_)

                nbv = [nbb, nbb2]
                invs2 = [sb2("invs2a", [128, 1]), sb2("invs2b", [128, 1])]
                m_invs = {}

                def partChain(tb):
                    p = tb % 2
                    em.wait("scalar", "S", m_xo_ev[(tb, 3)][1])
                    em.wait("vector", "S", m_xo_ev[(tb, 3)][1])
                    em.wait("scalar", "V", m_add.get(tb - 1, 0))
                    m_ln2[tb] = emit_ln(em, st2[p], xob[:, :], y2b[:, :], 128, C, eps5_t)
                    em.wait("vector", "S", m_ln2[tb])
                    em.wait("vector", "XA%d" % p, vx[tb])
                    em.wait("vector", "PK", pkh["v"])
                    em.wait("vector", "S", m_ln3.get(tb - 1, 0))
                    m_add[tb] = em.step("vector", "V", lambda e, p=p: e.tensor_add(
                        x2b[:, :], xbv[p][:, :], y2b[:, :]))
                    em.wait("gpsimd", "V", m_add[tb])
                    pkh["v"] = em.dma("gpsimd",
                                      x2buf[t0 + tb * 128:t0 + tb * 128 + 128, :],
                                      x2b[:], "PK")
                    dma_x2b(tb + 2)
                    em.wait("scalar", "V", m_add[tb])
                    m_ln3[tb] = emit_ln(em, st2[p], x2b[:, :], y2b[:, :], 128, C, eps5_t)
                    em.wait("vector", "S", m_ln3[tb])
                    em.wait("scalar", "V", m_nbb.get(tb - 1, 0))
                    m_rms[tb] = emit_ln(em, st2[p], y2b[:, :], y3b[:, :], 128, C, eps8_t,
                                        s_src=m_ln3[tb])
                    em.wait("vector", "S", m_rms[tb])
                    va1 = em.step("vector", "V", lambda e, p=p: e.tensor_reduce(
                        st2[p]["am"][:, :1], y3b[:, :], axis=AX.X, op=ALU.max,
                        apply_absolute_value=True))
                    em.wait("vector", "V", va1)
                    va2 = em.step("vector", "V", lambda e, p=p: e.reciprocal(
                        st2[p]["r0"][:, :1], st2[p]["am"][:, :1]))
                    em.wait("vector", "V", va2)
                    va3 = em.step("vector", "V", lambda e, p=p: e.tensor_scalar(
                        st2[p]["rstd"][:, :1], st2[p]["r0"][:, :1], 127.0, None, ALU.mult))
                    em.wait("vector", "V", va3)
                    em.wait("vector", "T", t_nT.get(tb - 2, 0))
                    m_nbb[tb] = 0
                    for c2 in (0, 1024):
                        vni = em.step("vector", "V", lambda e, p=p, c2=c2: e.tensor_scalar(
                            nib[:, :], y3b[:, c2:c2 + 1024],
                            st2[p]["rstd"][:, :1], None, ALU.mult))
                        em.wait("vector", "V", vni)
                        m_nbb[tb] = em.step("vector", "V", lambda e, p=p, c2=c2: e.tensor_copy(
                            nbv[p][:, c2:c2 + 1024], nib[:, :]))
                        em.wait("vector", "V", m_nbb[tb])
                    m_invs[tb] = em.step("vector", "V", lambda e, p=p: e.tensor_scalar(
                        invs2[p][:, :1], st2[p]["am"][:, :1], 1.0 / 127.0,
                        None, ALU.mult))
                def partTrs(tb):
                    p = tb % 2
                    em.wait("tensor", "V", m_invs[tb])
                    em.wait("tensor", "V", m_psr.get(tb - 1, 0))
                    vtr = em.step("tensor", "T", lambda e, p=p: e.transpose(
                        psr[:1, :128], invs2[p][:, :1], identf_t[:, :]))
                    em.wait("vector", "T", vtr)
                    m_psr[tb] = em.step("vector", "V", lambda e, tb=tb: e.tensor_copy(
                        srow[:1, tb * 128:(tb + 1) * 128], psr[:1, :128]))
                    em.wait("tensor", "V", m_nbb[tb])
                    carN['c'], _ = emit_trsN(em, [
                        ([(lambda e, b, c=8 * g + j, j=j, p=p: e.transpose(
                              b[:, j * 128:(j + 1) * 128],
                              nbv[p][:, c * 128:(c + 1) * 128], identb_t[:, :]))
                          for j in range(8)],
                         [lambda e, b, g=g, tb=tb: e.tensor_copy(
                             bass.AP(nT, (8 * g) * HALF + tb * 128,
                                     [[NF * HALF, 128], [HALF, 8], [1, 128]]),
                             bass.AP(b, 0, [[1024, 128], [128, 8], [1, 128]]))])
                        for g in range(2)], pstnA, pstnB, carry=carN['c'])
                    t_nT[tb] = em.val("T")

                for k in range(NTB + 2):
                    if 1 <= k <= NTB:
                        partEv(k - 1)
                    if k < NTB:
                        partA2b(k)
                    if 1 <= k <= NTB:
                        partChain(k - 1)
                    if k >= 2:
                        partTrs(k - 2)
                em.flush()
            gtstk.close()

            with nc.psum_tensor("psbc_h%d" % half, [128, 1024], F32) as psb_:
                em.run("tensor", lambda e: e.matmul(
                    psb_[:, 0:512], onesf_t[:1, :],
                    srow[:1, 0:512], start=True, stop=True))
                vt = em.step("tensor", "T", lambda e: e.matmul(
                    psb_[:, 512:1024], onesf_t[:1, :],
                    srow[:1, 512:1024], start=True, stop=True))
                em.wait("vector", "T", vt)
                em.run("vector", lambda e: e.tensor_copy(sbc[:, :], psb_[:, :]))
                em.flush()

            if dbg and half == 0:
                em.dma("gpsimd", d_nT[:, :], nT[:, :], "GP")
                em.dma("gpsimd", d_sbc[:, :], sbc[:, :], "GP")
                em.flush()

            with contextlib.ExitStack() as s3o:
                d_sb = s3o.enter_context(nc.sbuf_tensor("d_sb_h%d" % half, [128, 8 * C], F32))
                ws = contextlib.ExitStack()
                sb3 = lambda n, s, d=F32: ws.enter_context(
                    nc.sbuf_tensor("%s_h%d" % (n, half), s, d))
                stg = sb3("stg", [128, 3 * C])
                swg = sb3("swg", [128, 2 * C], BF16)
                swu = sb3("swu", [128, 2 * C], BF16)
                pth = sb3("pth", [128, NHB * HALF], BF16)
                ptl = sb3("ptl", [128, NHB * HALF], BF16)
                gsil = sb3("gsil", [128, 512])
                eg = sb3("eg", [128, 512])
                eu = sb3("eu", [128, 512])
                epb = sb3("epb", [128, 512])
                wstg = sb3("wstg", [128, 2 * 512])
                swd = sb3("swd", [128, 2 * 512], BF16)
                pst3 = contextlib.ExitStack()
                NPAIR = 3
                psg = [pst3.enter_context(nc.psum_tensor(
                    "psg%d_h%d" % (i, half), [128, 512], F32)) for i in range(NPAIR)]
                psu = [pst3.enter_context(nc.psum_tensor(
                    "psu%d_h%d" % (i, half), [128, 512], F32)) for i in range(NPAIR)]
                psdx = [pst3.enter_context(nc.psum_tensor(
                    "psdx%d_h%d" % (i, half), [128, 512], F32)) for i in range(8 - 2 * NPAIR)]
                psd = psg + psu + psdx

                em.fence_dma()
                sign_done = {}
                vepm = {}
                pair_T = {}
                pair_V = {}
                wsign = {}
                mmdone = {}
                mlast = {}
                evadd = {}

                def emit_up_load(lid):
                    hbg = lid // 2
                    if hbg >= H // 128:
                        return 0
                    wmat = wgTd if lid % 2 == 0 else wuTd
                    if lid >= 3:
                        em.wait("sync", "S", sign_done.get(lid - 3, 0))
                    return em.dma("sync", bass.AP(stg, (lid % 3) * C,
                                                  [[3 * C, 128], [1, C]]),
                                  wmat[hbg, :, :], "UP%d" % (lid % 3))
                up_ld = {}

                def emit_wd_load(wid):
                    if wid >= NSUP * 4 * NHB:
                        return 0
                    supw, rem = divmod(wid, 4 * NHB)
                    dmw, hcw = divmod(rem, NHB)
                    if wid >= 2:
                        em.wait("sync", "S", wsign.get(wid - 2, 0))
                    off = (supw * HS + hcw * 128) * C + dmw * 512
                    return em.dma("sync", wstg[:, (wid % 2) * 512:(wid % 2 + 1) * 512],
                                  bass.AP(wdTd, off, [[C, 128], [1, 512]]), "WD%d" % (wid % 2))
                wd_ld = {}

                for lid in range(2):
                    up_ld[lid] = emit_up_load(lid)

                def emit_up_sign(hbg2s):
                    if hbg2s >= NSUP * NHB:
                        return
                    l0s, l1s = 2 * hbg2s, 2 * hbg2s + 1
                    em.wait("scalar", "UP%d" % (l0s % 3), up_ld[l0s])
                    em.wait("scalar", "UP%d" % (l1s % 3), up_ld[l1s])
                    if hbg2s >= 2:
                        em.wait("scalar", "T", pair_T.get((hbg2s - 2, 1), 0))
                    sign_done[l0s] = em.step("scalar", "S",
                        lambda e, sl=(l0s % 3), so=(hbg2s % 2): e.activation(
                            swg[:, so * C:(so + 1) * C],
                            stg[:, sl * C:(sl + 1) * C], AF.Sign))
                    sign_done[l1s] = em.step("scalar", "S",
                        lambda e, sl=(l1s % 3), so=(hbg2s % 2): e.activation(
                            swu[:, so * C:(so + 1) * C],
                            stg[:, sl * C:(sl + 1) * C], AF.Sign))
                    up_ld[l0s + 2] = emit_up_load(l0s + 2)
                    up_ld[l1s + 2] = emit_up_load(l1s + 2)

                emit_up_sign(0)
                for sup in range(NSUP):
                    h0 = sup * HS
                    for hb in range(NHB):
                        hbg2 = sup * NHB + hb
                        l0, l1 = 2 * hbg2, 2 * hbg2 + 1
                        sdg = sign_done[l0]
                        sd = sign_done[l1]
                        em.wait("tensor", "S", sdg)
                        slot = hbg2 % 2
                        for tt in range(2):
                            gpair = hbg2 * 2 + tt
                            pr = gpair % NPAIR
                            if sup >= 1 and gpair - sup * NHB * 2 < NPAIR:
                                em.wait("tensor", "V", evadd[(sup * 4 - 1, NPAIR + pr)])
                            em.wait("tensor", "V", pair_V.get(gpair - NPAIR, 0))
                            for f in range(NF):
                                em.run("tensor", lambda e, f=f, pr=pr, tt=tt, slot=slot: e.matmul(
                                    psg[pr][:, :],
                                    swg[:, slot * C + f * 128:slot * C + (f + 1) * 128],
                                    nT[:, f * HALF + tt * 512:f * HALF + (tt + 1) * 512],
                                    start=(f == 0), stop=(f == NF - 1)))
                            if tt == 0:
                                em.wait("tensor", "S", sd)
                            for f in range(NF - 1):
                                em.run("tensor", lambda e, f=f, pr=pr, tt=tt, slot=slot: e.matmul(
                                    psu[pr][:, :],
                                    swu[:, slot * C + f * 128:slot * C + (f + 1) * 128],
                                    nT[:, f * HALF + tt * 512:f * HALF + (tt + 1) * 512],
                                    start=(f == 0), stop=False))
                            pair_T[(hbg2, tt)] = em.step("tensor", "T",
                                lambda e, pr=pr, tt=tt, slot=slot: e.matmul(
                                    psu[pr][:, :],
                                    swu[:, slot * C + (NF - 1) * 128:slot * C + NF * 128],
                                    nT[:, (NF - 1) * HALF + tt * 512:(NF - 1) * HALF + (tt + 1) * 512],
                                    start=False, stop=True))
                        emit_up_sign(hbg2 + 1)
                        for tt in range(2):
                            gpair = hbg2 * 2 + tt
                            pr = gpair % NPAIR
                            em.wait("vector", "T", pair_T[(hbg2, tt)])
                            va = em.step("vector", "V", lambda e, pr=pr, tt=tt: e.tensor_mul(
                                eg[:, :], psg[pr][:, :], sbc[:, tt * 512:(tt + 1) * 512]))
                            em.wait("scalar", "V", va)
                            vsl = em.step("scalar", "S", lambda e: e.activation(
                                gsil[:, :], eg[:, :], AF.Silu))
                            em.wait("vector", "V", vepm.get(gpair - 1, 0))
                            veu = em.step("vector", "V", lambda e, pr=pr, tt=tt: e.tensor_mul(
                                eu[:, :], psu[pr][:, :], sbc[:, tt * 512:(tt + 1) * 512]))
                            em.wait("vector", "S", vsl)
                            em.wait("vector", "V", veu)
                            em.wait("vector", "V", pair_V.get(gpair - 1, 0))
                            vep = em.step("vector", "V", lambda e: e.tensor_mul(
                                epb[:, :], gsil[:, :], eu[:, :]))
                            vepm[gpair] = vep
                            col = hb * HALF + tt * 512
                            em.wait("vector", "V", vep)
                            vph = em.step("vector", "V", lambda e, col=col: e.tensor_copy(
                                pth[:, col:col + 512], epb[:, :]))
                            em.wait("vector", "V", vph)
                            pair_V[gpair] = em.step("vector", "V",
                                lambda e, col=col: e.scalar_tensor_tensor(
                                    ptl[:, col:col + 512], epb[:, :], 1.0,
                                    pth[:, col:col + 512], ALU.mult, ALU.subtract))
                    if sup == 0:
                        for wid in range(2):
                            wd_ld[wid] = emit_wd_load(wid)

                    if dbg and half == 0 and sup == 0:
                        em.dma("gpsimd", d_pth[:, :], pth[:, :], "GP")
                        em.dma("gpsimd", d_ptl[:, :], ptl[:, :], "GP")
                        em.flush()

                    if True:
                        for dm in range(4):
                            dmg = sup * 4 + dm
                            for hc in range(NHB):
                                wid = dmg * NHB + hc
                                slot = wid % 2
                                em.wait("scalar", "WD%d" % (wid % 2), wd_ld[wid])
                                if wid >= 2:
                                    em.wait("scalar", "T", mmdone.get(wid - 2, 0))
                                wsign[wid] = em.step("scalar", "S",
                                    lambda e, slot=slot: e.activation(
                                        swd[:, slot * 512:(slot + 1) * 512],
                                        wstg[:, slot * 512:(slot + 1) * 512], AF.Sign))
                                wd_ld[wid + 2] = emit_wd_load(wid + 2)
                                em.wait("tensor", "S", wsign[wid])
                                tb_order = (6, 7, 0, 1, 2, 3, 4, 5)
                                for tb in tb_order:
                                    if hc == 0 and dmg >= 1:
                                        em.wait("tensor", "V", evadd.get((dmg - 1, tb), 0))
                                    if hc == 0 and dm == 0 and tb == 0:
                                        em.wait("tensor", "V",
                                                pair_V[(sup * NHB + NHB) * 2 - 1])
                                    for pi, pslab in enumerate((pth, ptl)):
                                        mk = (lambda e, hc=hc, tb=tb, pi=pi,
                                              pslab=pslab, slot=slot: e.matmul(
                                            psd[tb][:, :],
                                            pslab[:, hc * HALF + tb * 128:hc * HALF + (tb + 1) * 128],
                                            swd[:, slot * 512:(slot + 1) * 512],
                                            start=(hc == 0 and pi == 0),
                                            stop=(hc == NHB - 1 and pi == 1)))
                                        if pi == 1 and hc == NHB - 1:
                                            mlast[(dmg, tb)] = em.step("tensor", "T", mk)
                                        elif tb == 5 and pi == 1:
                                            mmdone[wid] = em.step("tensor", "T", mk)
                                        else:
                                            em.run("tensor", mk)
                                if hc == NHB - 1:
                                    mmdone[wid] = mlast[(dmg, 5)]
                            if dmg >= 4:
                                em.wait("vector", "V", evadd[(dmg - 4, NTB - 1)])
                            for tb in (6, 7, 0, 1, 2, 3, 4, 5):
                                em.wait("vector", "T", mlast[(dmg, tb)])
                                dcol = tb * C + dm * 512
                                if sup == 0:
                                    evadd[(dmg, tb)] = em.step("vector", "V",
                                        lambda e, tb=tb, dcol=dcol: e.tensor_copy(
                                            d_sb[:, dcol:dcol + 512], psd[tb][:, :]))
                                else:
                                    evadd[(dmg, tb)] = em.step("vector", "V",
                                        lambda e, tb=tb, dcol=dcol: e.tensor_tensor(
                                            d_sb[:, dcol:dcol + 512], psd[tb][:, :],
                                            d_sb[:, dcol:dcol + 512], ALU.add))

                em.flush()
                pst3.close()
                ws.close()
                with contextlib.ExitStack() as s4:
                    sb4 = lambda n, s, d=F32: s4.enter_context(
                        nc.sbuf_tensor("%s_h%d4" % (n, half), s, d))
                    xbl4 = [sb4("xbl4a", [128, C]), sb4("xbl4b", [128, C])]
                    l4v = [sb4("l4va", [128, C]), sb4("l4vb", [128, C])]
                    outb = sb4("outb", [128, C])
                    vout = 0
                    em.wait("sync", "PK", em.val("PK"))
                    em.fence_dma(engines=("vector", "scalar"))
                    vadd4 = {}
                    vx4 = {}
                    for tb in range(2):
                        vx4[tb] = em.dma("sync", xbl4[tb][:],
                                         x2buf[t0 + tb * 128:t0 + tb * 128 + 128, :],
                                         "XA%d" % (tb % 2))
                    for tb in range(NTB):
                        row0 = t0 + tb * 128
                        m4 = emit_ln(em, st2[tb % 2], d_sb[:, tb * C:(tb + 1) * C],
                                     l4v[tb % 2][:, :], 128, C, eps5_t)
                        em.wait("vector", "S", m4)
                        em.wait("vector", "XA%d" % (tb % 2), vx4[tb])
                        em.wait("vector", "OU", vout)
                        vo = em.step("vector", "V", lambda e, p=tb % 2: e.tensor_add(
                            outb[:, :], xbl4[p][:, :], l4v[p][:, :]))
                        vadd4[tb] = vo
                        em.wait("gpsimd", "V", vo)
                        vout = em.dma("gpsimd", ys[row0:row0 + 128, :], outb[:], "OU")
                        if tb + 2 < NTB:
                            em.wait("sync", "V", vo)
                            vx4[tb + 2] = em.dma(
                                "sync", xbl4[tb % 2][:],
                                x2buf[t0 + (tb + 2) * 128:t0 + (tb + 2) * 128 + 128, :],
                                "XA%d" % (tb % 2))
                    em.flush()

    em.wait("gpsimd", "OU", em.val("OU"))
    em.wait("sync", "OU", em.val("OU"))
    em.wait("gpsimd", "PK", em.val("PK"))
    em.wait("gpsimd", "GP", em.val("GP"))
    em.flush()
    gst.close()
    return nc


_host_cache = {}


def _host_consts():
    if _host_cache:
        return _host_cache
    identf = np.eye(128, dtype=np.float32)
    identb = np.eye(128).astype(ml_dtypes.bfloat16)
    onesf = np.ones((1, 128), dtype=np.float32)
    ll = np.arange(L)
    maskadd = np.where(ll[None, :] <= ll[:, None], 0.0, -1e9).astype(np.float32)
    binmask = (ll[None, :] <= ll[:, None]).astype(np.float32)
    j = np.arange(64, dtype=np.float64)
    inv_freq = 1.0 / (10000.0 ** (j * 2.0 / 128.0))
    t = np.arange(L, dtype=np.float64)
    ang = t[:, None] * inv_freq[None, :]
    cosv = np.cos(ang)
    sinv = np.sin(ang)
    ropeC = np.zeros((L, C), dtype=np.float32)
    ropeS = np.zeros((L, C), dtype=np.float32)
    for h in range(NH):
        ropeC[:, h * 128 + 0:h * 128 + 128:2] = cosv
        ropeC[:, h * 128 + 1:h * 128 + 128:2] = cosv
        ropeS[:, h * 128 + 0:h * 128 + 128:2] = -sinv
        ropeS[:, h * 128 + 1:h * 128 + 128:2] = sinv
    _host_cache.update(identf=identf, identb=identb, onesf=onesf,
                       maskadd=maskadd, binmask=binmask, ropeC=ropeC, ropeS=ropeS)
    return _host_cache


def kernel(L0_attn_x, L0_attn_latent_queries, L0_attn_Wk, L0_attn_Wv,
           L0_attn_Wg, L0_attn_Wp, L0_ffn_W_g, L0_ffn_W_u, L0_ffn_W_d):
    x = np.ascontiguousarray(np.asarray(L0_attn_x, dtype=np.float32))
    latq = np.asarray(L0_attn_latent_queries, dtype=np.float32)
    wk = np.ascontiguousarray(np.asarray(L0_attn_Wk, dtype=np.float32))
    wv = np.ascontiguousarray(np.asarray(L0_attn_Wv, dtype=np.float32))
    wg = np.ascontiguousarray(np.asarray(L0_attn_Wg, dtype=np.float32))
    wp = np.ascontiguousarray(np.asarray(L0_attn_Wp, dtype=np.float32))
    def tile_up(w):
        wt = np.asarray(w, dtype=np.float32).T
        t = wt.reshape(NF, 128, H // 128, 128)
        return np.ascontiguousarray(t.transpose(2, 1, 0, 3).reshape(H // 128, 128, C))
    wgT = tile_up(L0_ffn_W_g)
    wuT = tile_up(L0_ffn_W_u)
    wdT = np.ascontiguousarray(np.asarray(L0_ffn_W_d, dtype=np.float32).T)
    latqT = np.ascontiguousarray(latq[0].transpose(2, 1, 0).reshape(128, NH * L))
    cst = _host_consts()

    nc = build_program()
    in_maps = []
    for c in range(8):
        b = c // 4
        r0 = (c % 4) * TOK
        in_maps.append(dict(
            xs=np.ascontiguousarray(x[b, r0:r0 + TOK, :]),
            xh=np.ascontiguousarray(x[b, 0:L, :]),
            latqT=latqT, wk=wk, wv=wv, wp=wp, wga=wg,
            wgT=wgT, wuT=wuT, wdT=wdT,
            identf=cst["identf"], identb=cst["identb"], onesf=cst["onesf"],
            maskadd=cst["maskadd"], binmask=cst["binmask"],
            ropeC=cst["ropeC"], ropeS=cst["ropeS"]))
    res = run_bass_kernel_spmd(nc, in_maps, core_ids=list(range(8)))
    out = np.empty((2, 8192, C), dtype=np.float32)
    for c in range(8):
        b = c // 4
        r0 = (c % 4) * TOK
        out[b, r0:r0 + TOK, :] = res.results[c]["ys"]
    return out


if __name__ == "__main__":
    nc = build_program()
    print("program built ok")


# version 26 (final)
# speedup vs baseline: 1.1989x; 1.0056x over previous
import contextlib
import numpy as np
import ml_dtypes
import concourse.bass as bass
import concourse.mybir as mybir
from concourse.bass_utils import run_bass_kernel_spmd

F32 = mybir.dt.float32
BF16 = mybir.dt.bfloat16
I32 = mybir.dt.int32
AF = mybir.ActivationFunctionType
ALU = mybir.AluOpType
AX = mybir.AxisListType

C = 2048
H = 8192
NH = 16
L = 64
TOK = 2048
HALF = 1024
NTB = 8
NF = C // 128
SCALE = float(1.0 / np.sqrt(128.0))
EPS_LN = 1e-5
EPS_RMS = 1e-8
HS = 1024
NSUP = H // HS
NHB = HS // 128
ENGINES = ("sync", "scalar", "vector", "tensor", "gpsimd")


class EM:
    def __init__(self, nc, sems):
        self.nc = nc
        self.sems = sems
        self.q = {e: [] for e in ENGINES}
        self.cnt = {k: 0 for k in sems}

    def on(self, eng, fn):
        self.q[eng].append(fn)

    def val(self, name):
        return self.cnt[name]

    def wait(self, eng, name, v):
        if v <= 0:
            return
        s = self.sems[name]
        self.on(eng, lambda e: e.wait_ge(s, v))

    def dma(self, eng, dst, src, name="LD", **kw):
        self.cnt[name] += 16
        v = self.cnt[name]
        s = self.sems[name]
        self.on(eng, lambda e: e.dma_start(dst, src, **kw).then_inc(s, 16))
        return v

    def step(self, eng, name, build):
        self.cnt[name] += 1
        v = self.cnt[name]
        s = self.sems[name]
        self.on(eng, lambda e: build(e).then_inc(s, 1))
        return v

    def run(self, eng, build):
        self.on(eng, lambda e: build(e))

    def fence_dma(self, engines=("sync", "vector", "scalar", "gpsimd")):
        for nm in ("PK", "OU", "GP"):
            v = self.cnt[nm]
            if v:
                for eng in engines:
                    self.wait(eng, nm, v)

    def flush(self):
        q = self.q
        self.q = {e: [] for e in ENGINES}
        if not any(q.values()):
            return
        with self.nc.Block(no_gpsimd_drain=True) as blk:
            for eng in ENGINES:
                fns = q[eng]
                if fns:
                    def make(fl):
                        def body(e):
                            for f in fl:
                                f(e)
                        return body
                    getattr(blk, eng)(make(fns))


def emit_trsN(em, groups, psA, psB, carry=None):
    ca, cb = carry if carry else (0, 0)
    vcop = {-2: ca, -1: cb}
    for i, (tfns, vfns) in enumerate(groups):
        bank = psA if i % 2 == 0 else psB
        prev = vcop.get(i - 2, 0)
        if prev:
            em.wait("tensor", "V", prev)
        vt = 0
        for j, tf in enumerate(tfns):
            if j == len(tfns) - 1:
                vt = em.step("tensor", "T", lambda e, f=tf, b=bank: f(e, b))
            else:
                em.run("tensor", lambda e, f=tf, b=bank: f(e, b))
        em.wait("vector", "T", vt)
        mk = 0
        for j, vf in enumerate(vfns):
            if j > 0:
                em.wait("vector", "V", mk)
            mk = em.step("vector", "V", lambda e, f=vf, b=bank: f(e, b))
        vcop[i] = mk
    n = len(groups)
    lastA = vcop[n - 2] if n % 2 == 0 else vcop[n - 1]
    lastB = vcop[n - 1] if n % 2 == 0 else vcop[n - 2]
    return (lastA, lastB), vcop[n - 1]


def emit_trs(em, items, psA, psB, carry=None):
    ca, cb = carry if carry else (0, 0)
    vcop = {-2: ca, -1: cb}
    for i, (tfn, vfn) in enumerate(items):
        bank = psA if i % 2 == 0 else psB
        prev = vcop.get(i - 2, 0)
        if prev:
            em.wait("tensor", "V", prev)
        vt = em.step("tensor", "T", lambda e, f=tfn, b=bank: f(e, b))
        em.wait("vector", "T", vt)
        if isinstance(vfn, (list, tuple)):
            mk = 0
            for j, vf in enumerate(vfn):
                if j > 0:
                    em.wait("vector", "V", mk)
                mk = em.step("vector", "V", lambda e, f=vf, b=bank: f(e, b))
            vcop[i] = mk
        else:
            vcop[i] = em.step("vector", "V", lambda e, f=vfn, b=bank: f(e, b))
    n = len(items)
    lastA = vcop[n - 2] if n % 2 == 0 else vcop[n - 1]
    lastB = vcop[n - 1] if n % 2 == 0 else vcop[n - 2]
    return (lastA, lastB), vcop[n - 1]


def emit_ln(em, st, src, dst, nrows, ncols, eps_ap, s_src=0):
    r = slice(0, nrows)
    if s_src:
        em.wait("scalar", "S", s_src)
    vs = em.step("scalar", "S", lambda e: e.activation(
        st["trash"][r, :ncols], src, AF.Square, accum_out=st["sq"][r, :1]))
    v1 = em.step("vector", "V", lambda e: e.reduce_sum(st["sm"][r, :1], src, axis=AX.X))
    inv_n = 1.0 / ncols
    em.wait("scalar", "V", v1)
    s1 = em.step("scalar", "S", lambda e: e.mul(st["mm"][r, :1], st["sm"][r, :1], inv_n))
    em.wait("scalar", "S", vs)
    s2 = em.step("scalar", "S", lambda e: e.mul(st["vv"][r, :1], st["sq"][r, :1], inv_n))
    em.wait("scalar", "S", s1)
    s3 = em.step("scalar", "S", lambda e: e.activation(
        st["r0"][r, :1], st["mm"][r, :1], AF.Square))
    em.wait("scalar", "S", s3)
    s5 = em.step("scalar", "S", lambda e: e.activation(
        st["sq"][r, :1], st["r0"][r, :1], AF.Identity, bias=st["vv"][r, :1],
        scale=-1.0))
    em.wait("scalar", "S", s5)
    s6 = em.step("scalar", "S", lambda e: e.activation(
        st["r0"][r, :1], st["sq"][r, :1], AF.Sqrt, bias=eps_ap[r, :1]))
    em.wait("vector", "S", s6)
    v2 = em.step("vector", "V", lambda e: e.reciprocal(st["rstd"][r, :1], st["r0"][r, :1]))
    em.wait("scalar", "V", v2)
    s7 = em.step("scalar", "S", lambda e: e.mul(st["mm"][r, :1], st["mm"][r, :1], -1.0))
    em.wait("scalar", "S", s7)
    s8 = em.step("scalar", "S", lambda e: e.mul(st["nbias"][r, :1], st["rstd"][r, :1],
                                                st["mm"][r, :1]))
    em.wait("scalar", "S", s8)
    return em.step("scalar", "S", lambda e: e.activation(
        dst, src, AF.Identity, bias=st["nbias"][r, :1], scale=st["rstd"][r, :1]))


def build_program(dbg=False):
    nc = bass.Bass("TRN2", target_bir_lowering=False)
    DK = "ExternalOutput" if dbg else "Internal"

    xs = nc.dram_tensor("xs", [TOK, C], F32, kind="ExternalInput")
    xhd = nc.dram_tensor("xh", [L, C], F32, kind="ExternalInput")
    latqT = nc.dram_tensor("latqT", [128, NH * L], F32, kind="ExternalInput")
    wkd = nc.dram_tensor("wk", [C, C], F32, kind="ExternalInput")
    wvd = nc.dram_tensor("wv", [C, C], F32, kind="ExternalInput")
    wpd = nc.dram_tensor("wp", [C, C], F32, kind="ExternalInput")
    wgad = nc.dram_tensor("wga", [C, NH * L], F32, kind="ExternalInput")
    wgTd = nc.dram_tensor("wgT", [H // 128, 128, C], F32, kind="ExternalInput")
    wuTd = nc.dram_tensor("wuT", [H // 128, 128, C], F32, kind="ExternalInput")
    wdTd = nc.dram_tensor("wdT", [H, C], F32, kind="ExternalInput")
    identf = nc.dram_tensor("identf", [128, 128], F32, kind="ExternalInput")
    identb = nc.dram_tensor("identb", [128, 128], BF16, kind="ExternalInput")
    onesf = nc.dram_tensor("onesf", [1, 128], F32, kind="ExternalInput")
    maskadd = nc.dram_tensor("maskadd", [L, L], F32, kind="ExternalInput")
    binmask = nc.dram_tensor("binmask", [L, L], F32, kind="ExternalInput")
    ropeC = nc.dram_tensor("ropeC", [L, C], F32, kind="ExternalInput")
    ropeS = nc.dram_tensor("ropeS", [L, C], F32, kind="ExternalInput")

    ys = nc.dram_tensor("ys", [TOK, C], F32, kind="ExternalOutput")

    apbuf_hi = nc.dram_tensor("apbuf_hi", [NH * L, C], BF16, kind=DK)
    apbuf_lo = nc.dram_tensor("apbuf_lo", [NH * L, C], BF16, kind=DK)
    x2buf = nc.dram_tensor("x2buf", [TOK, C], F32, kind=DK)
    wgac = nc.dram_tensor("wgac", [2, C, 1024], BF16, kind="Internal")
    if dbg:
        d_y1h = nc.dram_tensor("d_y1h", [L, C], F32, kind="ExternalOutput")
        d_k = nc.dram_tensor("d_k", [L, C], F32, kind="ExternalOutput")
        d_attnT = nc.dram_tensor("d_attnT", [128, NH * L], F32, kind="ExternalOutput")
        d_gth = nc.dram_tensor("d_gth", [128, 8 * HALF], BF16, kind="ExternalOutput")
        d_nT = nc.dram_tensor("d_nT", [128, NF * HALF], BF16, kind="ExternalOutput")
        d_sbc = nc.dram_tensor("d_sbc", [128, HALF], F32, kind="ExternalOutput")
        d_pth = nc.dram_tensor("d_pth", [128, NHB * HALF], BF16, kind="ExternalOutput")
        d_ptl = nc.dram_tensor("d_ptl", [128, NHB * HALF], BF16, kind="ExternalOutput")

    sems = {k: nc.alloc_semaphore(name="sem_" + k)
            for k in ("LD", "GP", "T", "V", "S", "XA0", "XA1", "WG0", "WG1",
                      "UP0", "UP1", "UP2", "WD0", "WD1", "WP0", "WP1",
                      "KV0", "KV1", "PK", "OU", "GC")}
    em = EM(nc, sems)

    gst = contextlib.ExitStack()
    gsb = lambda n, s, d=F32: gst.enter_context(nc.sbuf_tensor(n, s, d))

    identf_t = gsb("identf_t", [128, 128])
    identb_t = gsb("identb_t", [128, 128], BF16)
    onesf_t = gsb("onesf_t", [1, 128])
    st = {k: gsb("st_" + k, [128, 1]) for k in
          ("sq", "sm", "mm", "vv", "r0", "rstd", "nbias", "am", "invs")}
    st["trash"] = gsb("st_trash", [128, C])
    eps5_t = gsb("eps5_t", [128, 1])
    eps8_t = gsb("eps8_t", [128, 1])
    stB = {k: gsb("stB_" + k, [128, 1]) for k in
           ("sq", "sm", "mm", "vv", "r0", "rstd", "nbias", "am", "invs")}
    stB["trash"] = st["trash"]
    st2 = [st, stB]

    v0 = None
    for dst, srck in [(identf_t[:], identf[:, :]), (identb_t[:], identb[:, :]),
                      (onesf_t[:], onesf[:, :])]:
        v0 = em.dma("sync", dst, srck, "LD")
    em.wait("sync", "LD", v0)
    em.run("vector", lambda e: e.memset(eps5_t[:, :1], EPS_LN))
    em.run("vector", lambda e: e.memset(eps8_t[:, :1], EPS_RMS))
    em.flush()

    with contextlib.ExitStack() as ps:
        psb = lambda n, s, d=F32: ps.enter_context(nc.sbuf_tensor(n, s, d))
        pps = lambda n, s, d=F32: ps.enter_context(nc.psum_tensor(n, s, d))
        maskadd_t = psb("maskadd_t", [L, L])
        binmask_t = psb("binmask_t", [L, L])
        ropeC_t = psb("ropeC_t", [L, C])
        ropeS_t = psb("ropeS_t", [L, C])
        xh_t = psb("xh_t", [L, C])
        y1h = psb("y1h", [L, C])
        y1hT = psb("y1hT", [128, NF * L])
        kbuf = psb("kbuf", [L, C])
        vbuf = psb("vbuf", [L, C])
        ksw = psb("ksw", [L, C])
        ktmp = psb("ktmp", [L, C])
        kT = psb("kT", [128, NH * L])
        qT = psb("qT", [128, NH * L])
        smb = psb("smb", [L, L])
        wb = psb("wb", [L, L])
        wTs = psb("wTs", [L, NH * L])
        attnT = psb("attnT", [128, NH * L])
        wchunk = psb("wchunk", [128, 2 * C])
        aph = psb("aph", [128, 8 * C], BF16)
        apl = psb("apl", [128, 8 * C], BF16)
        pbig = pps("pbig", [L, C])
        trstk = contextlib.ExitStack()
        pstA = trstk.enter_context(nc.psum_tensor("pstA", [128, 512], F32))
        pstB = trstk.enter_context(nc.psum_tensor("pstB", [128, 512], F32))

        for dst, srck in [(maskadd_t[:], maskadd[:, :]), (binmask_t[:], binmask[:, :]),
                          (ropeC_t[:], ropeC[:, :]), (ropeS_t[:], ropeS[:, :])]:
            em.dma("sync", dst, srck, "LD")
        em.dma("sync", xh_t[:], xhd[:, :], "LD")
        vl = em.dma("sync", qT[:, :], latqT[:, :], "LD")
        em.wait("scalar", "LD", vl)
        em.wait("vector", "LD", vl)
        mln1h = emit_ln(em, st, xh_t[:L, :], y1h[:L, :], L, C, eps5_t)
        em.wait("tensor", "S", mln1h)

        carP, vy1t = emit_trs(em, [
            (lambda e, b, c=c: e.transpose(
                b[:, :64], y1h[:L, c * 128:(c + 1) * 128], identf_t[:L, :L]),
             lambda e, b, c=c: e.tensor_copy(y1hT[:, c * L:(c + 1) * L], b[:, :64]))
            for c in range(NF)], pstA, pstB)
        em.wait("tensor", "V", vy1t)

        kv_T = {}
        for wi, (wmat, dst) in enumerate(((wkd, kbuf), (wvd, vbuf))):
            for f in range(NF):
                gi = wi * NF + f
                slot = (gi % 2) * C
                if gi >= 2:
                    em.wait("sync", "T", kv_T[gi - 2])
                vw = em.dma("sync", wchunk[:, slot:slot + C],
                            wmat[f * 128:(f + 1) * 128, :], "KV%d" % (gi % 2))
                em.wait("tensor", "KV%d" % (gi % 2), vw)
                for ot in range(4):
                    mk = (lambda e, f=f, ot=ot, slot=slot: e.matmul(
                        pbig[:L, ot * 512:(ot + 1) * 512],
                        y1hT[:, f * L:(f + 1) * L],
                        wchunk[:, slot + ot * 512:slot + (ot + 1) * 512],
                        start=(f == 0), stop=(f == NF - 1)))
                    if ot == 3:
                        kv_T[gi] = em.step("tensor", "T", mk)
                    else:
                        em.run("tensor", mk)
            em.wait("vector", "T", kv_T[wi * NF + NF - 1])
            vkv = em.step("vector", "V", lambda e, dst=dst: e.tensor_copy(
                dst[:L, :], pbig[:L, :]))
            em.wait("tensor", "V", vkv)
            if wi == 0:
                vkv_k = vkv

        em.wait("vector", "V", vkv_k)
        em.run("vector", lambda e: e.tensor_copy(
            bass.AP(ksw, 0, [[C, L], [2, C // 2]]),
            bass.AP(kbuf, 1, [[C, L], [2, C // 2]])))
        vr1 = em.step("vector", "V", lambda e: e.tensor_copy(
            bass.AP(ksw, 1, [[C, L], [2, C // 2]]),
            bass.AP(kbuf, 0, [[C, L], [2, C // 2]])))
        em.run("vector", lambda e: e.tensor_mul(ktmp[:L, :], kbuf[:L, :], ropeC_t[:L, :]))
        em.wait("vector", "V", vr1)
        vr2 = em.step("vector", "V", lambda e: e.tensor_mul(
            ksw[:L, :], ksw[:L, :], ropeS_t[:L, :]))
        em.wait("vector", "V", vr2)
        vrk = em.step("vector", "V", lambda e: e.tensor_add(
            kbuf[:L, :], ktmp[:L, :], ksw[:L, :]))
        em.wait("tensor", "V", vrk)

        carP, vkt = emit_trs(em, [
            (lambda e, b, c=c: e.transpose(
                b[:, :64], kbuf[:L, c * 128:(c + 1) * 128], identf_t[:L, :L]),
             lambda e, b, c=c: e.tensor_copy(kT[:, c * L:(c + 1) * L], b[:, :64]))
            for c in range(NF)], pstA, pstB, carry=carP)
        em.flush()

        trstk.close()
        hstk = contextlib.ExitStack()
        pss = hstk.enter_context(nc.psum_tensor("pss", [L, 512], F32))
        psa = hstk.enter_context(nc.psum_tensor("psa", [128, 512], F32))
        pss2 = hstk.enter_context(nc.psum_tensor("pss2", [L, 512], F32))
        psa2 = hstk.enter_context(nc.psum_tensor("psa2", [128, 512], F32))

        m_sc = {}
        m_at = {}
        t_sc = {}
        vs_prev = 0
        em.wait("tensor", "V", vkt)
        em.wait("tensor", "LD", vl)
        for h in range(NH):
            hsl = slice(h * L, (h + 1) * L)
            pb = pss if h % 2 == 0 else pss2
            pa = psa if h % 2 == 0 else psa2
            if h >= 2:
                em.wait("tensor", "V", m_sc[h - 2])
            vt = em.step("tensor", "T", lambda e, hsl=hsl, pb=pb: e.matmul(
                pb[:L, :L], qT[:, hsl], kT[:, hsl], start=True, stop=True))
            em.wait("vector", "T", vt)
            em.wait("vector", "S", vs_prev)
            vsm = em.step("vector", "V", lambda e, pb=pb: e.scalar_tensor_tensor(
                smb[:L, :L], pb[:L, :L], SCALE, maskadd_t[:L, :L], ALU.mult, ALU.add))
            em.wait("vector", "V", vsm)
            vv = em.step("vector", "V", lambda e: e.reduce_max(
                st["mm"][:L, :1], smb[:L, :L], axis=AX.X, negate=True))
            em.wait("scalar", "V", vv)
            vs = em.step("scalar", "S", lambda e: e.activation(
                wb[:L, :L], smb[:L, :L], AF.Exp, bias=st["mm"][:L, :1]))
            vs_prev = vs
            em.wait("vector", "S", vs)
            vbm = em.step("vector", "V", lambda e: e.tensor_mul(
                wb[:L, :L], wb[:L, :L], binmask_t[:L, :L]))
            em.wait("vector", "V", vbm)
            vrs = em.step("vector", "V", lambda e: e.reduce_sum(
                st["sm"][:L, :1], wb[:L, :L], axis=AX.X))
            em.wait("vector", "V", vrs)
            vrc = em.step("vector", "V", lambda e: e.reciprocal(
                st["r0"][:L, :1], st["sm"][:L, :1]))
            em.wait("vector", "V", vrc)
            vv2 = em.step("vector", "V", lambda e: e.tensor_scalar(
                wb[:L, :L], wb[:L, :L], st["r0"][:L, :1], None, ALU.mult))
            em.wait("tensor", "V", vv2)
            vt2 = em.step("tensor", "T", lambda e, pb=pb: e.transpose(
                pb[:L, 64:64 + L], wb[:L, :L], identf_t[:L, :L]))
            t_sc[h] = vt2
            em.wait("vector", "T", vt2)
            m_sc[h] = em.step("vector", "V", lambda e, hsl=hsl, pb=pb: e.tensor_copy(
                wTs[:L, hsl], pb[:L, 64:64 + L]))
            em.wait("tensor", "V", m_sc[h])
            if h >= 2:
                em.wait("tensor", "V", m_at[h - 2])
            vt3 = em.step("tensor", "T", lambda e, h=h, hsl=hsl, pa=pa: e.matmul(
                pa[:, :L], vbuf[:L, h * 128:(h + 1) * 128], wTs[:L, hsl],
                start=True, stop=True))
            em.wait("vector", "T", vt3)
            m_at[h] = em.step("vector", "V", lambda e, hsl=hsl, pa=pa: e.tensor_copy(
                attnT[:, hsl], pa[:, :L]))
        em.wait("tensor", "V", m_at[NH - 1])

        hstk.close()
        pbig2 = pps("pbig2", [L, C])
        pbv = [pbig, pbig2]
        m_ap = {}
        s_ap = {}
        wp_ld = {}
        for h in range(NH):
            slot = (h % 2) * C
            if h >= 2:
                em.wait("sync", "T", tconsumed[h - 2]) if False else None
            wp_ld[h] = em.dma("sync", wchunk[:, slot:slot + C],
                              wpd[h * 128:(h + 1) * 128, :], "WP%d" % (h % 2))
            if h == 1:
                break
        for h in range(NH):
            slot = (h % 2) * C
            em.wait("tensor", "WP%d" % (h % 2), wp_ld[h])
            if h >= 2:
                em.wait("tensor", "V", m_ap[h - 2])
                em.wait("tensor", "S", s_ap[h - 2])
            prow = (h % 2) * 64
            pcol = (h // 2) * C
            vt = None
            pb_ = pbv[h % 2]
            for dm in range(4):
                mk = (lambda e, h=h, dm=dm, slot=slot, pb_=pb_: e.matmul(
                    pb_[:L, dm * 512:(dm + 1) * 512],
                    attnT[:, h * L:(h + 1) * L],
                    wchunk[:, slot + dm * 512:slot + (dm + 1) * 512],
                    start=True, stop=True))
                if dm == 3:
                    vt = em.step("tensor", "T", mk)
                else:
                    em.run("tensor", mk)
            if h + 2 < NH:
                em.wait("sync", "T", vt)
                wp_ld[h + 2] = em.dma("sync", wchunk[:, ((h + 2) % 2) * C:((h + 2) % 2) * C + C],
                                      wpd[(h + 2) * 128:(h + 3) * 128, :], "WP%d" % (h % 2))
            em.wait("scalar", "T", vt)
            s_ap[h] = em.step("scalar", "S", lambda e, prow=prow, pcol=pcol, pb_=pb_: e.copy(
                aph[prow:prow + L, pcol:pcol + C], pb_[:L, :]))
            em.wait("vector", "S", s_ap[h])
            em.wait("vector", "T", vt)
            m_ap[h] = em.step("vector", "V", lambda e, prow=prow, pcol=pcol, pb_=pb_: e.scalar_tensor_tensor(
                apl[prow:prow + L, pcol:pcol + C], pb_[:L, :], 1.0,
                aph[prow:prow + L, pcol:pcol + C], ALU.mult, ALU.subtract))
        for j in range(8):
            em.wait("gpsimd", "V", m_ap[2 * j + 1])
            em.wait("gpsimd", "S", s_ap[2 * j + 1])
            dj = [[C, 128], [1, C]]
            em.dma("gpsimd", bass.AP(apbuf_hi, j * 128 * C, dj),
                   bass.AP(aph, j * C, [[8 * C, 128], [1, C]]), "GP")
            em.dma("gpsimd", bass.AP(apbuf_lo, j * 128 * C, dj),
                   bass.AP(apl, j * C, [[8 * C, 128], [1, C]]), "GP")
        em.flush()

    out_dma_total = 0
    for half in range(2):
        t0 = half * HALF
        with contextlib.ExitStack() as hst:
            hsb = lambda n, s, d=F32: hst.enter_context(
                nc.sbuf_tensor("%s_h%d" % (n, half), s, d))
            nT = hsb("nT", [128, NF * HALF], BF16)
            sbc = hsb("sbc", [128, HALF])
            srow = hsb("srow", [1, HALF])
            gtstk = contextlib.ExitStack()
            gsb2 = lambda n, s, d=F32: gtstk.enter_context(
                nc.sbuf_tensor("%s_h%d" % (n, half), s, d))
            gth = gsb2("gth", [128, 8 * HALF], BF16)
            gtl = gsb2("gtl", [128, 8 * HALF], BF16)
            gth_ref = gth

            em.fence_dma()
            with contextlib.ExitStack() as s2:
                sb2 = lambda n, s, d=F32: s2.enter_context(
                    nc.sbuf_tensor("%s_h%da" % (n, half), s, d))
                pp2 = lambda n, s, d=F32: s2.enter_context(
                    nc.psum_tensor("%s_h%da" % (n, half), s, d))
                wgah = sb2("wgah", [128, NF * 1024], BF16)
                wgal = sb2("wgal", [128, NF * 1024], BF16)
                wstg = sb2("wstg", [128, 2 * 1024])
                xblk = sb2("xblk", [128, C])
                xblk2 = sb2("xblk2", [128, C])
                y1 = sb2("y1", [128, C])
                y1b = sb2("y1b", [128, C])
                y1Th = sb2("y1Th", [128, C], BF16)
                y1Tl = sb2("y1Tl", [128, C], BF16)
                ebuf = sb2("ebuf", [128, 1024])
                gbuf = sb2("gbuf", [128, 1024])
                red = sb2("red", [128, 16])
                pstA2 = pp2("pstA2", [128, 512])
                pstB2 = pp2("pstB2", [128, 512])
                pslg = pp2("pslg", [128, 1024])
                pslg2 = pp2("pslg2", [128, 1024])

                m_wsp = {}
                wga_ld = 0
                slab_d = [[16 * 1024, 128], [1024, 16], [1, 1024]]
                dram_d = [[1024, 128], [128 * 1024, 16], [1, 1024]]
                if half == 0:
                    for f in range(NF):
                        slot = (f % 2) * 1024
                        if f >= 2:
                            em.wait("sync", "V", m_wsp[f - 2])
                        vw = em.dma("sync", wstg[:, slot:slot + 1024],
                                    wgad[f * 128:(f + 1) * 128, :], "WG%d" % (f % 2))
                        em.wait("vector", "WG%d" % (f % 2), vw)
                        vwc = em.step("vector", "V", lambda e, f=f, slot=slot, wstg=wstg: e.tensor_copy(
                            wgah[:, f * 1024:(f + 1) * 1024], wstg[:, slot:slot + 1024]))
                        em.wait("vector", "V", vwc)
                        m_wsp[f] = em.step("vector", "V", lambda e, f=f, slot=slot, wstg=wstg: e.scalar_tensor_tensor(
                            wgal[:, f * 1024:(f + 1) * 1024], wstg[:, slot:slot + 1024], 1.0,
                            wgah[:, f * 1024:(f + 1) * 1024], ALU.mult, ALU.subtract))
                    em.wait("gpsimd", "V", m_wsp[NF - 1])
                    em.dma("gpsimd", bass.AP(wgac, 0, dram_d),
                           bass.AP(wgah, 0, slab_d), "GP")
                    em.dma("gpsimd", bass.AP(wgac, C * 1024, dram_d),
                           bass.AP(wgal, 0, slab_d), "GP")
                else:
                    wga_ld0 = em.dma("sync", bass.AP(wgah, 0, slab_d),
                                     bass.AP(wgac, 0, dram_d), "WG0")
                    wga_ld = em.dma("sync", bass.AP(wgal, 0, slab_d),
                                    bass.AP(wgac, C * 1024, dram_d), "WG1")
                    for f in range(NF):
                        m_wsp[f] = 0

                xbv = [xblk, xblk2]
                y1v = [y1, y1b]
                y1Thv = [y1Th, y1Th]
                y1Tlv = [y1Tl, y1Tl]
                ebv = [ebuf, ebuf]
                gbv = [gbuf, gbuf]
                redv = [red, red]
                pslgv = [pslg, pslg2]
                m_aff = {}; m_sp = {}; m_lg = {}; m_rm = {}; m_ex = {}
                m_gb = {}; t_y1 = {}; t_gt = {}
                vx = {}
                carH = {'c': None}

                def dma_x2a(tb):
                    if tb >= NTB:
                        return
                    p = tb % 2
                    if tb >= 2:
                        em.wait("sync", "S", m_aff[tb - 2])
                    vx[tb] = em.dma("sync", xbv[p][:],
                                    xs[t0 + tb * 128:t0 + tb * 128 + 128, :], "XA%d" % p)
                dma_x2a(0)
                dma_x2a(1)

                def partLN(tb):
                    if tb >= NTB:
                        return
                    p = tb % 2
                    em.wait("scalar", "XA%d" % p, vx[tb])
                    em.wait("vector", "XA%d" % p, vx[tb])
                    if tb >= 2:
                        em.wait("scalar", "T", t_y1.get(tb - 2, 0))
                    m_aff[tb] = emit_ln(em, st2[p], xbv[p][:, :], y1v[p][:, :], 128, C, eps5_t)
                    dma_x2a(tb + 2)

                def partTrs2a(tb):
                    p = tb % 2
                    em.wait("tensor", "S", m_aff[tb])
                    if tb >= 1:
                        em.wait("vector", "T", m_lg.get(tb - 1, 0))
                    carH['c'], m_sp[tb] = emit_trsN(em, [
                        ([(lambda e, b, c=4 * g + j, j=j, p=p: e.transpose(
                              b[:, j * 128:(j + 1) * 128],
                              y1v[p][:, c * 128:(c + 1) * 128], identf_t[:, :]))
                          for j in range(4)],
                         [lambda e, b, c0=g * 512, p=p: e.tensor_copy(
                             y1Thv[p][:, c0:c0 + 512], b[:, :512]),
                          lambda e, b, c0=g * 512, p=p: e.scalar_tensor_tensor(
                             y1Tlv[p][:, c0:c0 + 512], b[:, :512], 1.0,
                             y1Thv[p][:, c0:c0 + 512], ALU.mult, ALU.subtract)])
                        for g in range(4)], pstA2, pstB2, carry=carH['c'])
                    t_y1[tb] = em.val("T")
                def partA2(tb):
                    p = tb % 2
                    em.wait("tensor", "V", m_sp[tb])
                    if tb >= 2:
                        em.wait("tensor", "S", m_ex.get(tb - 2, 0))
                        em.wait("tensor", "V", m_rm.get(tb - 2, 0))
                    for ot in range(2):
                        for f in range(NF):
                            if tb == 0 and ot == 0:
                                if half == 0:
                                    em.wait("tensor", "V", m_wsp[f])
                                elif f == 0:
                                    em.wait("tensor", "WG0", wga_ld0)
                                    em.wait("tensor", "WG1", wga_ld)
                            for pp_, (lh, rh) in enumerate(
                                    ((y1Thv[p], wgah), (y1Thv[p], wgal), (y1Tlv[p], wgah))):
                                mk = (lambda e, ot=ot, f=f, pp_=pp_, lh=lh, rh=rh, p=p: e.matmul(
                                    pslgv[p][:, ot * 512:(ot + 1) * 512],
                                    lh[:, f * 128:(f + 1) * 128],
                                    rh[:, f * 1024 + ot * 512:f * 1024 + (ot + 1) * 512],
                                    start=(f == 0 and pp_ == 0),
                                    stop=(f == NF - 1 and pp_ == 2)))
                                if ot == 1 and f == NF - 1 and pp_ == 2:
                                    m_lg[tb] = em.step("tensor", "T", mk)
                                else:
                                    em.run("tensor", mk)

                def partBsm(tb):
                    p = tb % 2
                    em.wait("vector", "T", m_lg[tb])
                    em.wait("vector", "S", m_ex.get(tb - 1, 0))
                    m_rm[tb] = em.step("vector", "V", lambda e, p=p: e.reduce_max(
                        bass.AP(redv[p], 0, [[16, 128], [1, 16], [0, 1]]),
                        bass.AP(pslgv[p], 0, [[1024, 128], [64, 16], [1, 64]]),
                        axis=AX.X, negate=True))
                    em.wait("vector", "V", m_rm[tb])
                    vng = em.step("vector", "V", lambda e, p=p: e.tensor_scalar(
                        redv[p][:, :16], redv[p][:, :16], SCALE, None, ALU.mult))
                    em.wait("scalar", "V", vng)
                    em.wait("scalar", "V", m_gb.get(tb - 1, 0))
                    for g in range(16):
                        mk = (lambda e, g=g, p=p: e.activation(
                            ebv[p][:, g * 64:(g + 1) * 64], pslgv[p][:, g * 64:(g + 1) * 64],
                            AF.Exp, bias=redv[p][:, g:g + 1], scale=SCALE))
                        if g == 15:
                            m_ex[tb] = em.step("scalar", "S", mk)
                        else:
                            em.run("scalar", mk)
                    em.wait("vector", "S", m_ex[tb])
                    vsu = em.step("vector", "V", lambda e, p=p: e.reduce_sum(
                        bass.AP(redv[p], 0, [[16, 128], [1, 16], [0, 1]]),
                        bass.AP(ebv[p], 0, [[1024, 128], [64, 16], [1, 64]]), axis=AX.X))
                    em.wait("vector", "V", vsu)
                    vrc = em.step("vector", "V", lambda e, p=p: e.reciprocal(
                        redv[p][:, :16], redv[p][:, :16]))
                    em.wait("vector", "V", vrc)
                    if tb >= 1:
                        em.wait("vector", "T", t_gt.get(tb - 1, 0))
                    for g in range(16):
                        mk = (lambda e, g=g, p=p: e.tensor_scalar(
                            gbv[p][:, g * 64:(g + 1) * 64], ebv[p][:, g * 64:(g + 1) * 64],
                            redv[p][:, g:g + 1], None, ALU.mult))
                        if g == 15:
                            m_gb[tb] = em.step("vector", "V", mk)
                        else:
                            em.run("vector", mk)
                def partBgt(tb):
                    p = tb % 2
                    em.wait("tensor", "V", m_gb[tb])
                    def gt_dst(t, g, tb=tb):
                        return bass.AP(t, (4 * g) * HALF + tb * 128,
                                       [[8 * HALF, 128], [HALF, 4], [1, 128]])
                    bsrc = lambda b: bass.AP(b, 0, [[512, 128], [128, 4], [1, 128]])
                    carH['c'], _ = emit_trsN(em, [
                        ([(lambda e, b, q=4 * g + j, j=j, p=p: e.transpose(
                              b[:, j * 128:(j + 1) * 128],
                              gbv[p][:, q * 128:(q + 1) * 128], identf_t[:, :]))
                          for j in range(4)],
                         [lambda e, b, g=g, f_=gt_dst, s_=bsrc: e.tensor_copy(
                              f_(gth, g), s_(b)),
                          lambda e, b, g=g, f_=gt_dst, s_=bsrc: e.scalar_tensor_tensor(
                              f_(gtl, g), s_(b), 1.0, f_(gth, g),
                              ALU.mult, ALU.subtract)])
                        for g in range(2)], pstA2, pstB2, carry=carH['c'])
                    t_gt[tb] = em.val("T")

                partLN(0)
                for k in range(NTB + 1):
                    partLN(k + 1)
                    if k < NTB:
                        partTrs2a(k)
                    if k >= 1:
                        partBsm(k - 1)
                    if k < NTB:
                        partA2(k)
                    if k >= 1:
                        partBgt(k - 1)
                v2a_end = em.val("V")
                s2a_end = em.val("S")
                t2a_lg = m_lg[NTB - 1]
                s2a_aff = m_aff[NTB - 1]
                v2a_wsp = m_wsp.get(NF - 1, 0)

            if dbg and half == 0:
                em.dma("gpsimd", d_gth[:, :], gth_ref[:, :], "GP")
                em.flush()

            with contextlib.ExitStack() as s2:
                sb2 = lambda n, s, d=F32: s2.enter_context(
                    nc.sbuf_tensor("%s_h%db" % (n, half), s, d))
                pp2 = lambda n, s, d=F32: s2.enter_context(
                    nc.psum_tensor("%s_h%db" % (n, half), s, d))
                aph = sb2("aph", [128, 8 * C], BF16)
                apl = sb2("apl", [128, 8 * C], BF16)
                xblk = sb2("xblk", [128, C])
                xblk2 = sb2("xblk2", [128, C])
                xob = sb2("xob", [128, C])
                x2b = sb2("x2b", [128, C])
                y2b = sb2("y2b", [128, C])
                y3b = sb2("y3b", [128, C])
                nib = sb2("nib", [128, 1024], I32)
                nbb = sb2("nbb", [128, C], BF16)
                nbb2 = sb2("nbb2", [128, C], BF16)
                psxo = pp2("psxo", [128, C])
                pstnA = pp2("pstnA", [128, 1024], BF16)
                pstnB = pp2("pstnB", [128, 1024], BF16)
                psr = pp2("psr", [128, 512])

                dsta = [[8 * C, 128], [C, 4], [1, C]]
                srca = [[C, 128], [128 * C, 4], [1, C]]
                em.wait("sync", "GP", em.val("GP"))
                em.wait("sync", "T", t2a_lg)
                em.wait("sync", "V", v2a_wsp)
                em.dma("sync", bass.AP(aph, 0, dsta), bass.AP(apbuf_hi, 0, srca), "LD")
                vap1 = em.dma("sync", bass.AP(apl, 0, dsta), bass.AP(apbuf_lo, 0, srca), "LD")
                em.dma("sync", bass.AP(aph, 4 * C, dsta),
                       bass.AP(apbuf_hi, 4 * 128 * C, srca), "KV0")
                vap2 = em.dma("sync", bass.AP(apl, 4 * C, dsta),
                              bass.AP(apbuf_lo, 4 * 128 * C, srca), "KV0")

                xbv = [xblk, xblk2]
                pkh = {'v': 0}
                m_add = {}; m_ln2 = {}; m_ln3 = {}; m_rms = {}
                m_xo_mm = {}; m_xo_ev = {}; m_nbb = {}; m_psr = {}
                t_nT = {}
                vx = {}
                carN = {'c': None}

                def dma_x2b(tb):
                    if tb >= NTB:
                        return
                    p = tb % 2
                    if tb < 2:
                        em.wait("sync", "S", s2a_aff)
                        em.wait("sync", "V", v2a_wsp)
                    if tb >= 2:
                        em.wait("sync", "V", m_add.get(tb - 2, 0))
                    vx[tb] = em.dma("sync", xbv[p][:],
                                    xs[t0 + tb * 128:t0 + tb * 128 + 128, :], "XA%d" % p)
                dma_x2b(0)
                dma_x2b(1)

                def partA2b(tb):
                    p = tb % 2
                    for dm in range(4):
                        if tb >= 1:
                            nm_, mk_ = m_xo_ev[(tb - 1, dm)]
                            em.wait("tensor", nm_, mk_)
                        for q in range(8):
                            col = q * HALF + tb * 128
                            if tb == 0 and dm == 0:
                                if q == 0:
                                    em.wait("tensor", "V", v2a_end)
                                    em.wait("tensor", "S", s2a_end)
                                    em.wait("tensor", "LD", vap1)
                                elif q == 4:
                                    em.wait("tensor", "KV0", vap2)
                            for pp_, (lh, rh) in enumerate(
                                    ((gth, aph), (gth, apl), (gtl, aph))):
                                mk = (lambda e, dm=dm, q=q, pp_=pp_, lh=lh,
                                      rh=rh, col=col: e.matmul(
                                    psxo[:, dm * 512:(dm + 1) * 512],
                                    lh[:, col:col + 128],
                                    rh[:, q * C + dm * 512:q * C + (dm + 1) * 512],
                                    start=(q == 0 and pp_ == 0), stop=(q == 7 and pp_ == 2)))
                                if q == 7 and pp_ == 2:
                                    m_xo_mm[(tb, dm)] = em.step("tensor", "T", mk)
                                else:
                                    em.run("tensor", mk)

                def partEv(tb):
                    em.wait("scalar", "S", m_ln2.get(tb - 1, 0))
                    for dm in range(4):
                        em.wait("scalar", "T", m_xo_mm[(tb, dm)])
                        mk_ = em.step("scalar", "S", lambda e, dm=dm: e.copy(
                            xob[:, dm * 512:(dm + 1) * 512],
                            psxo[:, dm * 512:(dm + 1) * 512]))
                        m_xo_ev[(tb, dm)] = ("S", mk_)

                nbv = [nbb, nbb2]
                invs2 = [sb2("invs2a", [128, 1]), sb2("invs2b", [128, 1])]
                m_invs = {}

                def partChain(tb):
                    p = tb % 2
                    em.wait("scalar", "S", m_xo_ev[(tb, 3)][1])
                    em.wait("vector", "S", m_xo_ev[(tb, 3)][1])
                    em.wait("scalar", "V", m_add.get(tb - 1, 0))
                    m_ln2[tb] = emit_ln(em, st2[p], xob[:, :], y2b[:, :], 128, C, eps5_t)
                    em.wait("vector", "S", m_ln2[tb])
                    em.wait("vector", "XA%d" % p, vx[tb])
                    em.wait("vector", "PK", pkh["v"])
                    em.wait("vector", "S", m_ln3.get(tb - 1, 0))
                    m_add[tb] = em.step("vector", "V", lambda e, p=p: e.tensor_add(
                        x2b[:, :], xbv[p][:, :], y2b[:, :]))
                    em.wait("gpsimd", "V", m_add[tb])
                    pkh["v"] = em.dma("gpsimd",
                                      x2buf[t0 + tb * 128:t0 + tb * 128 + 128, :],
                                      x2b[:], "PK")
                    dma_x2b(tb + 2)
                    em.wait("scalar", "V", m_add[tb])
                    m_ln3[tb] = emit_ln(em, st2[p], x2b[:, :], y2b[:, :], 128, C, eps5_t)
                    em.wait("vector", "S", m_ln3[tb])
                    em.wait("scalar", "V", m_nbb.get(tb - 1, 0))
                    m_rms[tb] = emit_ln(em, st2[p], y2b[:, :], y3b[:, :], 128, C, eps8_t,
                                        s_src=m_ln3[tb])
                    em.wait("vector", "S", m_rms[tb])
                    va1 = em.step("vector", "V", lambda e, p=p: e.tensor_reduce(
                        st2[p]["am"][:, :1], y3b[:, :], axis=AX.X, op=ALU.max,
                        apply_absolute_value=True))
                    em.wait("vector", "V", va1)
                    va2 = em.step("vector", "V", lambda e, p=p: e.reciprocal(
                        st2[p]["r0"][:, :1], st2[p]["am"][:, :1]))
                    em.wait("vector", "V", va2)
                    va3 = em.step("vector", "V", lambda e, p=p: e.tensor_scalar(
                        st2[p]["rstd"][:, :1], st2[p]["r0"][:, :1], 127.0, None, ALU.mult))
                    em.wait("vector", "V", va3)
                    em.wait("vector", "T", t_nT.get(tb - 2, 0))
                    m_nbb[tb] = 0
                    for c2 in (0, 1024):
                        vni = em.step("vector", "V", lambda e, p=p, c2=c2: e.tensor_scalar(
                            nib[:, :], y3b[:, c2:c2 + 1024],
                            st2[p]["rstd"][:, :1], None, ALU.mult))
                        em.wait("vector", "V", vni)
                        m_nbb[tb] = em.step("vector", "V", lambda e, p=p, c2=c2: e.tensor_copy(
                            nbv[p][:, c2:c2 + 1024], nib[:, :]))
                        em.wait("vector", "V", m_nbb[tb])
                    m_invs[tb] = em.step("vector", "V", lambda e, p=p: e.tensor_scalar(
                        invs2[p][:, :1], st2[p]["am"][:, :1], 1.0 / 127.0,
                        None, ALU.mult))
                def partTrs(tb):
                    p = tb % 2
                    em.wait("tensor", "V", m_invs[tb])
                    em.wait("tensor", "V", m_psr.get(tb - 1, 0))
                    vtr = em.step("tensor", "T", lambda e, p=p: e.transpose(
                        psr[:1, :128], invs2[p][:, :1], identf_t[:, :]))
                    em.wait("vector", "T", vtr)
                    m_psr[tb] = em.step("vector", "V", lambda e, tb=tb: e.tensor_copy(
                        srow[:1, tb * 128:(tb + 1) * 128], psr[:1, :128]))
                    em.wait("tensor", "V", m_nbb[tb])
                    carN['c'], _ = emit_trsN(em, [
                        ([(lambda e, b, c=8 * g + j, j=j, p=p: e.transpose(
                              b[:, j * 128:(j + 1) * 128],
                              nbv[p][:, c * 128:(c + 1) * 128], identb_t[:, :]))
                          for j in range(8)],
                         [lambda e, b, g=g, tb=tb: e.tensor_copy(
                             bass.AP(nT, (8 * g) * HALF + tb * 128,
                                     [[NF * HALF, 128], [HALF, 8], [1, 128]]),
                             bass.AP(b, 0, [[1024, 128], [128, 8], [1, 128]]))])
                        for g in range(2)], pstnA, pstnB, carry=carN['c'])
                    t_nT[tb] = em.val("T")

                for k in range(NTB + 2):
                    if 1 <= k <= NTB:
                        partEv(k - 1)
                    if k < NTB:
                        partA2b(k)
                    if 1 <= k <= NTB:
                        partChain(k - 1)
                    if k >= 2:
                        partTrs(k - 2)
                v2b_end = em.val("V")
                s2b_end = em.val("S")
                t2b_mm = t_nT[NTB - 1]
                s2b_ev = m_xo_ev[(NTB - 1, 3)][1]
                v2b_psr = m_psr[NTB - 1]
            gtstk.close()

            with nc.psum_tensor("psbc_h%d" % half, [128, 1024], F32) as psb_:
                em.wait("tensor", "S", s2b_ev)
                em.wait("tensor", "V", v2b_psr)
                em.run("tensor", lambda e: e.matmul(
                    psb_[:, 0:512], onesf_t[:1, :],
                    srow[:1, 0:512], start=True, stop=True))
                vt = em.step("tensor", "T", lambda e: e.matmul(
                    psb_[:, 512:1024], onesf_t[:1, :],
                    srow[:1, 512:1024], start=True, stop=True))
                em.wait("vector", "T", vt)
                v_sbc = em.step("vector", "V", lambda e: e.tensor_copy(
                    sbc[:, :], psb_[:, :]))

            if dbg and half == 0:
                em.dma("gpsimd", d_nT[:, :], nT[:, :], "GP")
                em.dma("gpsimd", d_sbc[:, :], sbc[:, :], "GP")
                em.flush()

            with contextlib.ExitStack() as s3o:
                d_sb = s3o.enter_context(nc.sbuf_tensor("d_sb_h%d" % half, [128, 8 * C], F32))
                ws = contextlib.ExitStack()
                sb3 = lambda n, s, d=F32: ws.enter_context(
                    nc.sbuf_tensor("%s_h%d" % (n, half), s, d))
                stg = sb3("stg", [128, 3 * C])
                swg = sb3("swg", [128, 2 * C], BF16)
                swu = sb3("swu", [128, 2 * C], BF16)
                pth = sb3("pth", [128, NHB * HALF], BF16)
                ptl = sb3("ptl", [128, NHB * HALF], BF16)
                gsil = sb3("gsil", [128, 512])
                eg = sb3("eg", [128, 512])
                eu = sb3("eu", [128, 512])
                epb = sb3("epb", [128, 512])
                wstg = sb3("wstg", [128, 2 * 512])
                swd = sb3("swd", [128, 2 * 512], BF16)
                pst3 = contextlib.ExitStack()
                NPAIR = 3
                psg = [pst3.enter_context(nc.psum_tensor(
                    "psg%d_h%d" % (i, half), [128, 512], F32)) for i in range(NPAIR)]
                psu = [pst3.enter_context(nc.psum_tensor(
                    "psu%d_h%d" % (i, half), [128, 512], F32)) for i in range(NPAIR)]
                psdx = [pst3.enter_context(nc.psum_tensor(
                    "psdx%d_h%d" % (i, half), [128, 512], F32)) for i in range(8 - 2 * NPAIR)]
                psd = psg + psu + psdx

                em.fence_dma()
                em.wait("sync", "T", t2b_mm)
                em.wait("sync", "V", v_sbc)
                em.wait("tensor", "V", v_sbc)
                em.wait("tensor", "S", s2b_end)
                sign_done = {}
                vepm = {}
                pair_T = {}
                pair_V = {}
                wsign = {}
                mmdone = {}
                mlast = {}
                evadd = {}

                def emit_up_load(lid):
                    hbg = lid // 2
                    if hbg >= H // 128:
                        return 0
                    wmat = wgTd if lid % 2 == 0 else wuTd
                    if lid >= 3:
                        em.wait("sync", "S", sign_done.get(lid - 3, 0))
                    return em.dma("sync", bass.AP(stg, (lid % 3) * C,
                                                  [[3 * C, 128], [1, C]]),
                                  wmat[hbg, :, :], "UP%d" % (lid % 3))
                up_ld = {}

                def emit_wd_load(wid):
                    if wid >= NSUP * 4 * NHB:
                        return 0
                    supw, rem = divmod(wid, 4 * NHB)
                    dmw, hcw = divmod(rem, NHB)
                    if wid >= 2:
                        em.wait("sync", "S", wsign.get(wid - 2, 0))
                    off = (supw * HS + hcw * 128) * C + dmw * 512
                    return em.dma("sync", wstg[:, (wid % 2) * 512:(wid % 2 + 1) * 512],
                                  bass.AP(wdTd, off, [[C, 128], [1, 512]]), "WD%d" % (wid % 2))
                wd_ld = {}

                for lid in range(2):
                    up_ld[lid] = emit_up_load(lid)

                def emit_up_sign(hbg2s):
                    if hbg2s >= NSUP * NHB:
                        return
                    l0s, l1s = 2 * hbg2s, 2 * hbg2s + 1
                    em.wait("scalar", "UP%d" % (l0s % 3), up_ld[l0s])
                    em.wait("scalar", "UP%d" % (l1s % 3), up_ld[l1s])
                    if hbg2s >= 2:
                        em.wait("scalar", "T", pair_T.get((hbg2s - 2, 1), 0))
                    sign_done[l0s] = em.step("scalar", "S",
                        lambda e, sl=(l0s % 3), so=(hbg2s % 2): e.activation(
                            swg[:, so * C:(so + 1) * C],
                            stg[:, sl * C:(sl + 1) * C], AF.Sign))
                    sign_done[l1s] = em.step("scalar", "S",
                        lambda e, sl=(l1s % 3), so=(hbg2s % 2): e.activation(
                            swu[:, so * C:(so + 1) * C],
                            stg[:, sl * C:(sl + 1) * C], AF.Sign))
                    up_ld[l0s + 2] = emit_up_load(l0s + 2)
                    up_ld[l1s + 2] = emit_up_load(l1s + 2)

                emit_up_sign(0)
                for sup in range(NSUP):
                    h0 = sup * HS
                    for hb in range(NHB):
                        hbg2 = sup * NHB + hb
                        l0, l1 = 2 * hbg2, 2 * hbg2 + 1
                        sdg = sign_done[l0]
                        sd = sign_done[l1]
                        em.wait("tensor", "S", sdg)
                        slot = hbg2 % 2
                        for tt in range(2):
                            gpair = hbg2 * 2 + tt
                            pr = gpair % NPAIR
                            if sup >= 1 and gpair - sup * NHB * 2 < NPAIR:
                                em.wait("tensor", "V", evadd[(sup * 4 - 1, NPAIR + pr)])
                            em.wait("tensor", "V", pair_V.get(gpair - NPAIR, 0))
                            for f in range(NF):
                                em.run("tensor", lambda e, f=f, pr=pr, tt=tt, slot=slot: e.matmul(
                                    psg[pr][:, :],
                                    swg[:, slot * C + f * 128:slot * C + (f + 1) * 128],
                                    nT[:, f * HALF + tt * 512:f * HALF + (tt + 1) * 512],
                                    start=(f == 0), stop=(f == NF - 1)))
                            if tt == 0:
                                em.wait("tensor", "S", sd)
                            for f in range(NF - 1):
                                em.run("tensor", lambda e, f=f, pr=pr, tt=tt, slot=slot: e.matmul(
                                    psu[pr][:, :],
                                    swu[:, slot * C + f * 128:slot * C + (f + 1) * 128],
                                    nT[:, f * HALF + tt * 512:f * HALF + (tt + 1) * 512],
                                    start=(f == 0), stop=False))
                            pair_T[(hbg2, tt)] = em.step("tensor", "T",
                                lambda e, pr=pr, tt=tt, slot=slot: e.matmul(
                                    psu[pr][:, :],
                                    swu[:, slot * C + (NF - 1) * 128:slot * C + NF * 128],
                                    nT[:, (NF - 1) * HALF + tt * 512:(NF - 1) * HALF + (tt + 1) * 512],
                                    start=False, stop=True))
                        emit_up_sign(hbg2 + 1)
                        for tt in range(2):
                            gpair = hbg2 * 2 + tt
                            pr = gpair % NPAIR
                            em.wait("vector", "T", pair_T[(hbg2, tt)])
                            va = em.step("vector", "V", lambda e, pr=pr, tt=tt: e.tensor_mul(
                                eg[:, :], psg[pr][:, :], sbc[:, tt * 512:(tt + 1) * 512]))
                            em.wait("scalar", "V", va)
                            vsl = em.step("scalar", "S", lambda e: e.activation(
                                gsil[:, :], eg[:, :], AF.Silu))
                            em.wait("vector", "V", vepm.get(gpair - 1, 0))
                            veu = em.step("vector", "V", lambda e, pr=pr, tt=tt: e.tensor_mul(
                                eu[:, :], psu[pr][:, :], sbc[:, tt * 512:(tt + 1) * 512]))
                            em.wait("vector", "S", vsl)
                            em.wait("vector", "V", veu)
                            em.wait("vector", "V", pair_V.get(gpair - 1, 0))
                            vep = em.step("vector", "V", lambda e: e.tensor_mul(
                                epb[:, :], gsil[:, :], eu[:, :]))
                            vepm[gpair] = vep
                            col = hb * HALF + tt * 512
                            em.wait("vector", "V", vep)
                            vph = em.step("vector", "V", lambda e, col=col: e.tensor_copy(
                                pth[:, col:col + 512], epb[:, :]))
                            em.wait("vector", "V", vph)
                            pair_V[gpair] = em.step("vector", "V",
                                lambda e, col=col: e.scalar_tensor_tensor(
                                    ptl[:, col:col + 512], epb[:, :], 1.0,
                                    pth[:, col:col + 512], ALU.mult, ALU.subtract))
                    if sup == 0:
                        for wid in range(2):
                            wd_ld[wid] = emit_wd_load(wid)

                    if dbg and half == 0 and sup == 0:
                        em.dma("gpsimd", d_pth[:, :], pth[:, :], "GP")
                        em.dma("gpsimd", d_ptl[:, :], ptl[:, :], "GP")
                        em.flush()

                    if True:
                        for dm in range(4):
                            dmg = sup * 4 + dm
                            for hc in range(NHB):
                                wid = dmg * NHB + hc
                                slot = wid % 2
                                em.wait("scalar", "WD%d" % (wid % 2), wd_ld[wid])
                                if wid >= 2:
                                    em.wait("scalar", "T", mmdone.get(wid - 2, 0))
                                wsign[wid] = em.step("scalar", "S",
                                    lambda e, slot=slot: e.activation(
                                        swd[:, slot * 512:(slot + 1) * 512],
                                        wstg[:, slot * 512:(slot + 1) * 512], AF.Sign))
                                wd_ld[wid + 2] = emit_wd_load(wid + 2)
                                em.wait("tensor", "S", wsign[wid])
                                tb_order = (6, 7, 0, 1, 2, 3, 4, 5)
                                for tb in tb_order:
                                    if hc == 0 and dmg >= 1:
                                        em.wait("tensor", "V", evadd.get((dmg - 1, tb), 0))
                                    if hc == 0 and dm == 0 and tb == 0:
                                        em.wait("tensor", "V",
                                                pair_V[(sup * NHB + NHB) * 2 - 1])
                                    for pi, pslab in enumerate((pth, ptl)):
                                        mk = (lambda e, hc=hc, tb=tb, pi=pi,
                                              pslab=pslab, slot=slot: e.matmul(
                                            psd[tb][:, :],
                                            pslab[:, hc * HALF + tb * 128:hc * HALF + (tb + 1) * 128],
                                            swd[:, slot * 512:(slot + 1) * 512],
                                            start=(hc == 0 and pi == 0),
                                            stop=(hc == NHB - 1 and pi == 1)))
                                        if pi == 1 and hc == NHB - 1:
                                            mlast[(dmg, tb)] = em.step("tensor", "T", mk)
                                        elif tb == 5 and pi == 1:
                                            mmdone[wid] = em.step("tensor", "T", mk)
                                        else:
                                            em.run("tensor", mk)
                                if hc == NHB - 1:
                                    mmdone[wid] = mlast[(dmg, 5)]
                            if dmg >= 4:
                                em.wait("vector", "V", evadd[(dmg - 4, NTB - 1)])
                            for tb in (6, 7, 0, 1, 2, 3, 4, 5):
                                em.wait("vector", "T", mlast[(dmg, tb)])
                                dcol = tb * C + dm * 512
                                if sup == 0:
                                    evadd[(dmg, tb)] = em.step("vector", "V",
                                        lambda e, tb=tb, dcol=dcol: e.tensor_copy(
                                            d_sb[:, dcol:dcol + 512], psd[tb][:, :]))
                                else:
                                    evadd[(dmg, tb)] = em.step("vector", "V",
                                        lambda e, tb=tb, dcol=dcol: e.tensor_tensor(
                                            d_sb[:, dcol:dcol + 512], psd[tb][:, :],
                                            d_sb[:, dcol:dcol + 512], ALU.add))

                em.flush()
                pst3.close()
                ws.close()
                with contextlib.ExitStack() as s4:
                    sb4 = lambda n, s, d=F32: s4.enter_context(
                        nc.sbuf_tensor("%s_h%d4" % (n, half), s, d))
                    xbl4 = [sb4("xbl4a", [128, C]), sb4("xbl4b", [128, C])]
                    l4v = [sb4("l4va", [128, C]), sb4("l4vb", [128, C])]
                    outb = sb4("outb", [128, C])
                    vout = 0
                    em.wait("sync", "PK", em.val("PK"))
                    em.fence_dma(engines=("vector", "scalar"))
                    vadd4 = {}
                    vx4 = {}
                    for tb in range(2):
                        vx4[tb] = em.dma("sync", xbl4[tb][:],
                                         x2buf[t0 + tb * 128:t0 + tb * 128 + 128, :],
                                         "XA%d" % (tb % 2))
                    for tb in range(NTB):
                        row0 = t0 + tb * 128
                        m4 = emit_ln(em, st2[tb % 2], d_sb[:, tb * C:(tb + 1) * C],
                                     l4v[tb % 2][:, :], 128, C, eps5_t)
                        em.wait("vector", "S", m4)
                        em.wait("vector", "XA%d" % (tb % 2), vx4[tb])
                        em.wait("vector", "OU", vout)
                        vo = em.step("vector", "V", lambda e, p=tb % 2: e.tensor_add(
                            outb[:, :], xbl4[p][:, :], l4v[p][:, :]))
                        vadd4[tb] = vo
                        em.wait("gpsimd", "V", vo)
                        vout = em.dma("gpsimd", ys[row0:row0 + 128, :], outb[:], "OU")
                        if tb + 2 < NTB:
                            em.wait("sync", "V", vo)
                            vx4[tb + 2] = em.dma(
                                "sync", xbl4[tb % 2][:],
                                x2buf[t0 + (tb + 2) * 128:t0 + (tb + 2) * 128 + 128, :],
                                "XA%d" % (tb % 2))
                    em.flush()

    em.wait("gpsimd", "OU", em.val("OU"))
    em.wait("sync", "OU", em.val("OU"))
    em.wait("gpsimd", "PK", em.val("PK"))
    em.wait("gpsimd", "GP", em.val("GP"))
    em.flush()
    gst.close()
    return nc


_host_cache = {}


def _host_consts():
    if _host_cache:
        return _host_cache
    identf = np.eye(128, dtype=np.float32)
    identb = np.eye(128).astype(ml_dtypes.bfloat16)
    onesf = np.ones((1, 128), dtype=np.float32)
    ll = np.arange(L)
    maskadd = np.where(ll[None, :] <= ll[:, None], 0.0, -1e9).astype(np.float32)
    binmask = (ll[None, :] <= ll[:, None]).astype(np.float32)
    j = np.arange(64, dtype=np.float64)
    inv_freq = 1.0 / (10000.0 ** (j * 2.0 / 128.0))
    t = np.arange(L, dtype=np.float64)
    ang = t[:, None] * inv_freq[None, :]
    cosv = np.cos(ang)
    sinv = np.sin(ang)
    ropeC = np.zeros((L, C), dtype=np.float32)
    ropeS = np.zeros((L, C), dtype=np.float32)
    for h in range(NH):
        ropeC[:, h * 128 + 0:h * 128 + 128:2] = cosv
        ropeC[:, h * 128 + 1:h * 128 + 128:2] = cosv
        ropeS[:, h * 128 + 0:h * 128 + 128:2] = -sinv
        ropeS[:, h * 128 + 1:h * 128 + 128:2] = sinv
    _host_cache.update(identf=identf, identb=identb, onesf=onesf,
                       maskadd=maskadd, binmask=binmask, ropeC=ropeC, ropeS=ropeS)
    return _host_cache


def kernel(L0_attn_x, L0_attn_latent_queries, L0_attn_Wk, L0_attn_Wv,
           L0_attn_Wg, L0_attn_Wp, L0_ffn_W_g, L0_ffn_W_u, L0_ffn_W_d):
    x = np.ascontiguousarray(np.asarray(L0_attn_x, dtype=np.float32))
    latq = np.asarray(L0_attn_latent_queries, dtype=np.float32)
    wk = np.ascontiguousarray(np.asarray(L0_attn_Wk, dtype=np.float32))
    wv = np.ascontiguousarray(np.asarray(L0_attn_Wv, dtype=np.float32))
    wg = np.ascontiguousarray(np.asarray(L0_attn_Wg, dtype=np.float32))
    wp = np.ascontiguousarray(np.asarray(L0_attn_Wp, dtype=np.float32))
    def tile_up(w):
        wt = np.asarray(w, dtype=np.float32).T
        t = wt.reshape(NF, 128, H // 128, 128)
        return np.ascontiguousarray(t.transpose(2, 1, 0, 3).reshape(H // 128, 128, C))
    wgT = tile_up(L0_ffn_W_g)
    wuT = tile_up(L0_ffn_W_u)
    wdT = np.ascontiguousarray(np.asarray(L0_ffn_W_d, dtype=np.float32).T)
    latqT = np.ascontiguousarray(latq[0].transpose(2, 1, 0).reshape(128, NH * L))
    cst = _host_consts()

    nc = build_program()
    in_maps = []
    for c in range(8):
        b = c // 4
        r0 = (c % 4) * TOK
        in_maps.append(dict(
            xs=np.ascontiguousarray(x[b, r0:r0 + TOK, :]),
            xh=np.ascontiguousarray(x[b, 0:L, :]),
            latqT=latqT, wk=wk, wv=wv, wp=wp, wga=wg,
            wgT=wgT, wuT=wuT, wdT=wdT,
            identf=cst["identf"], identb=cst["identb"], onesf=cst["onesf"],
            maskadd=cst["maskadd"], binmask=cst["binmask"],
            ropeC=cst["ropeC"], ropeS=cst["ropeS"]))
    res = run_bass_kernel_spmd(nc, in_maps, core_ids=list(range(8)))
    out = np.empty((2, 8192, C), dtype=np.float32)
    for c in range(8):
        b = c // 4
        r0 = (c % 4) * TOK
        out[b, r0:r0 + TOK, :] = res.results[c]["ys"]
    return out


if __name__ == "__main__":
    nc = build_program()
    print("program built ok")
